# version 06
# baseline (speedup 1.0000x reference)
import contextlib
import numpy as np
import concourse.bass as bass
import concourse.mybir as mybir
from concourse.bass import IndirectOffsetOnAxis
from concourse.bass_utils import run_bass_kernel_spmd

F32 = mybir.dt.float32
BF16 = mybir.dt.bfloat16
U32 = mybir.dt.uint32
AF = mybir.ActivationFunctionType
ALU = mybir.AluOpType
AX = mybir.AxisListType

N = 4096
D = 1024
NH = 16
HD = 64
E = 8
FF = 4096
NT = 512
NTILES = N // NT
SUBS = NT // 128
HALF = N // 2

C = 1536
CT = C // NT
ECAP = 8 * C
BIG = float(1 << 20)
NSUB = N // 128

N_SCALES, N_TAPS = 11, 4
OFFSETS = [(1 << j) * tau for j in range(N_SCALES) for tau in range(N_TAPS)]
DIST = sorted(set(OFFSETS))
DIST_IDX = {o: i for i, o in enumerate(DIST)}
assert len(DIST) == 24

EPS = 1e-5
QSCALE = HD ** -0.5

DEBUG = False
ENGINES = ("sync", "gpsimd", "vector", "scalar", "tensor")


class P:

    def __init__(self, nc, es, semnames):
        self.nc = nc
        self.q = {e: [] for e in ENGINES}
        self.h = {s: es.enter_context(nc.semaphore(s)) for s in semnames}
        self.n = {s: 0 for s in semnames}

    def tok(self, sem):
        return (sem, self.n[sem])

    @staticmethod
    def _flat(toks, acc):
        if toks is None:
            return
        if isinstance(toks, tuple) and len(toks) == 2 and isinstance(toks[0], str):
            acc.append(toks)
            return
        for t in toks:
            P._flat(t, acc)

    def wait(self, eng, toks):
        acc = []
        P._flat(toks, acc)
        best = {}
        for sem, v in acc:
            best[sem] = max(best.get(sem, 0), v)
        for sem, v in best.items():
            assert v <= self.n[sem], f"forward wait on {sem}: {v} > {self.n[sem]}"
            if v > 0:
                self.q[eng].append(("w", sem, v))

    def fwait(self, eng, sem, val):
        self.q[eng].append(("w", sem, val))

    def op(self, eng, fn, inc=None, inck=1):
        self.q[eng].append(("o", fn, inc, inck))
        if inc is not None:
            self.n[inc] += inck
        return self.tok(inc) if inc else None

    def dma(self, eng, out, in_, sem, **kw):
        return self.op(eng, lambda e, o=out, i=in_, k=dict(kw): e.dma_start(out=o, in_=i, **k),
                       inc=sem, inck=16)

    def vchain(self, eng, fns, sem):
        t = None
        for fn in fns:
            if t is not None:
                self.wait(eng, t)
            t = self.op(eng, fn, inc=sem)
        return t

    def barrier(self, sems=None):
        if sems is None:
            sems = [sm for sm in self.n if sm not in ("qW", "sSC", "sXC")]
        toks = [(sm, self.n[sm]) for sm in sems]
        for eng in ENGINES:
            self.wait(eng, toks)

    def check_deadlock(self):
        pos = {e: 0 for e in ENGINES}
        cnt = {sm: 0 for sm in self.n}
        progress = True
        while progress:
            progress = False
            for eng in ENGINES:
                while pos[eng] < len(self.q[eng]):
                    it = self.q[eng][pos[eng]]
                    if it[0] == "w":
                        if cnt[it[1]] < it[2]:
                            break
                    else:
                        _, fn, inc, inck = it
                        if inc is not None:
                            cnt[inc] += inck
                    pos[eng] += 1
                    progress = True
        stuck = {e: (pos[e], len(self.q[e])) for e in ENGINES if pos[e] < len(self.q[e])}
        if stuck:
            for e, (i, n) in stuck.items():
                it = self.q[e][i]
                print(f"DEADLOCK {e} at op {i}/{n}: waiting {it[1]} >= {it[2]}, have {cnt[it[1]]}")
            raise AssertionError(f"deadlock: {stuck}")

    def emit(self, blk):
        for eng in ENGINES:
            ops = self.q[eng]
            if not ops:
                continue
            section = getattr(blk, eng)

            def body(e, ops=ops):
                for it in ops:
                    if it[0] == "w":
                        e.wait_ge(self.h[it[1]], it[2])
                    else:
                        _, fn, inc, inck = it
                        ins = fn(e)
                        if inc is not None:
                            ins.then_inc(self.h[inc], inck)
            section(body)


def build_nc(debug=False, phases=("A1", "IDX", "A2a", "B1", "B2", "B3"), races=False):
    nc = bass.Bass("TRN2", target_bir_lowering=False, detect_race_conditions=races)

    x = nc.dram_tensor("x", [N, D], F32, kind="ExternalInput").ap()
    w_qkv = nc.dram_tensor("w_qkv", [D, 3 * D], F32, kind="ExternalInput").ap()
    b_qkv = nc.dram_tensor("b_qkv", [3 * D], F32, kind="ExternalInput").ap()
    w_out = nc.dram_tensor("w_out", [D, D], F32, kind="ExternalInput").ap()
    b_out = nc.dram_tensor("b_out", [D], F32, kind="ExternalInput").ap()
    w_gate = nc.dram_tensor("w_gate", [D, D], F32, kind="ExternalInput").ap()
    b_gate = nc.dram_tensor("b_gate", [D], F32, kind="ExternalInput").ap()
    pos_bias = nc.dram_tensor("pos_bias", [44, NH], F32, kind="ExternalInput").ap()
    router_w = nc.dram_tensor("router_w", [E, D], F32, kind="ExternalInput").ap()
    w1 = nc.dram_tensor("w1", [E, FF, D], F32, kind="ExternalInput").ap()
    b1 = nc.dram_tensor("b1", [E, FF], F32, kind="ExternalInput").ap()
    w2 = nc.dram_tensor("w2", [E, D, FF], F32, kind="ExternalInput").ap()
    b2 = nc.dram_tensor("b2", [E, D], F32, kind="ExternalInput").ap()
    out = nc.dram_tensor("out", [N, D], F32, kind="ExternalOutput").ap()

    w1_bf = nc.dram_tensor("w1_bf", [E, FF, D], BF16, kind="Internal").ap()
    w2_bf = nc.dram_tensor("w2_bf", [E, D, FF], BF16, kind="Internal").ap()
    wq_perm = nc.dram_tensor("wq_perm", [8, 8, 128, 128], BF16, kind="Internal").ap()
    wk_perm = nc.dram_tensor("wk_perm", [8, 8, 128, 128], BF16, kind="Internal").ap()
    wv_perm = nc.dram_tensor("wv_perm", [8, 8, 128, 128], BF16, kind="Internal").ap()
    wg_perm = nc.dram_tensor("wg_perm", [8, 8, 128, 128], BF16, kind="Internal").ap()
    xn_sp = nc.dram_tensor("xn_sp", [N, D], BF16, kind="Internal").ap()
    gav_sp = nc.dram_tensor("gav_sp", [8, 128, N], BF16, kind="Internal").ap()
    pb_sp = nc.dram_tensor("pb_sp", [NH * 44], F32, kind="Internal").ap()
    expert_x = nc.dram_tensor("expert_x", [ECAP, D], BF16, kind="Internal").ap()
    y_sp = nc.dram_tensor("y_sp", [ECAP, D], BF16, kind="Internal").ap()
    attn_sp = nc.dram_tensor("attn_sp", [8, 128, N], BF16, kind="Internal").ap()
    cnt_dram = nc.dram_tensor("cnt_dram", [NSUB, E], F32, kind="Internal").ap()
    base_dram = nc.dram_tensor("base_dram", [NSUB, E], F32, kind="Internal").ap()

    dbg = {}
    if debug:
        dbg["xn"] = nc.dram_tensor("dbg_xn", [N, D], F32, kind="ExternalOutput").ap()
        dbg["k"] = nc.dram_tensor("dbg_k", [8, 128, N], F32, kind="ExternalOutput").ap()
        dbg["v"] = nc.dram_tensor("dbg_v", [8, 128, N], F32, kind="ExternalOutput").ap()
        dbg["coef"] = nc.dram_tensor("dbg_coef", [E, N], F32, kind="ExternalOutput").ap()
        dbg["gav"] = nc.dram_tensor("dbg_gav", [8, 128, N], F32, kind="ExternalOutput").ap()
        dbg["sc"] = nc.dram_tensor("dbg_sc", [128, NH, 44], F32, kind="ExternalOutput").ap()
        dbg["q"] = nc.dram_tensor("dbg_q", [8, 128, NT], F32, kind="ExternalOutput").ap()
        dbg["av"] = nc.dram_tensor("dbg_av", [8, 128, NT], F32, kind="ExternalOutput").ap()
        dbg["qkt"] = nc.dram_tensor("dbg_qkt", [128, SUBS, 8, 80], F32, kind="ExternalOutput").ap()
        dbg["qks"] = nc.dram_tensor("dbg_qks", [80, NT], F32, kind="ExternalOutput").ap()

    es = contextlib.ExitStack()
    with es:
        sb = nc.sbuf_tensor
        ident_f = es.enter_context(sb("ident_f", [128, 128], F32))
        ident_b = es.enter_context(sb("ident_b", [128, 128], BF16))
        sel_b = es.enter_context(sb("sel_b", [128, NH], BF16))
        sel2_b = es.enter_context(sb("sel2_b", [80, 128], BF16))
        pb_bc = es.enter_context(sb("pb_bc", [128, NH, 44], F32))
        rwT = es.enter_context(sb("rwT", [128, 8, E], F32))
        coef_bf = es.enter_context(sb("coef_bf", [E, N], BF16))
        bias_q = es.enter_context(sb("bias_q", [128, 8], F32))
        bias_k = es.enter_context(sb("bias_k", [128, 8], F32))
        bias_v = es.enter_context(sb("bias_v", [128, 8], F32))
        bias_g = es.enter_context(sb("bias_g", [128, 8], F32))
        bias_o = es.enter_context(sb("bias_o", [128, 8], F32))
        bias_b1 = es.enter_context(sb("bias_b1", [128, E, 32], F32))
        b2_bf = es.enter_context(sb("b2_bf", [E, D], BF16))
        ones_f = es.enter_context(sb("ones_f", [1, 128], F32))
        ones_b = es.enter_context(sb("ones_b", [1, 128], BF16))
        selt = es.enter_context(sb("selt", [128, NH], BF16))
        sel2t = es.enter_context(sb("sel2t", [NH, 128], BF16))
        wab = es.enter_context(sb("wab", [128, NSUB, 2], F32))
        idx1u = es.enter_context(sb("idx1u", [128, NSUB], U32))
        idx2u = es.enter_context(sb("idx2u", [128, NSUB], U32))
        xn_sc = [es.enter_context(sb("xn_sc0", [128, D], BF16))]
        kv_stack = contextlib.ExitStack()
        k_fm = kv_stack.enter_context(sb("k_fm", [128, 8, N], BF16))
        v_fm = kv_stack.enter_context(sb("v_fm", [128, 8, N], BF16))
        route_stack = contextlib.ExitStack()
        Lx = route_stack.enter_context(sb("Lx", [128, 128], F32))
        ones_col = route_stack.enter_context(sb("ones_col", [128, 1], F32))
        thr_row = route_stack.enter_context(sb("thr_row", [128, E], F32))
        mask1_all = route_stack.enter_context(sb("mask1_all", [128, NSUB, E], BF16))
        mask2_all = route_stack.enter_context(sb("mask2_all", [128, NSUB, E], BF16))
        pos_all = route_stack.enter_context(sb("pos_all", [128, NSUB, E], BF16))
        cnt_nat = route_stack.enter_context(sb("cnt_nat", [1, NSUB, E], F32))
        base_bc = route_stack.enter_context(sb("base_bc", [128, NSUB, E], F32))
        idx1f = route_stack.enter_context(sb("idx1f", [128, NSUB], F32))
        idx2f = route_stack.enter_context(sb("idx2f", [128, NSUB], F32))
        eoff32 = route_stack.enter_context(sb("eoff32", [32, E], F32))

        p = P(nc, es, ["qW", "tT", "tV", "tC", "tP", "sP0A", "sP0B", "sBIAS", "sWPERM",
                       "sWS0", "sWS1", "sX0", "sX1", "sWKV0", "sWKV1", "sXNSP",
                       "sGAV", "sCF", "sWO", "sWG0", "sWG1", "sDBG", "sDBG2", "sB2",
                       "sXNL0", "sXNL1", "sGV0", "sGV1", "sOUT",
                       "sIDX", "sXC", "sSC", "sEX0", "sEX1", "sY", "sGA", "sAS", "sAL"])
        blk = es.enter_context(nc.Block("mk"))

        def c_ident(t):
            def f(e, t=t):
                e.memset(t[:], 0.0)
                return e.affine_select(out=t[:], in_=t[:], compare_op=ALU.not_equal,
                                       fill=1.0, base=0, pattern=[[-1, 128]],
                                       channel_multiplier=1)
            return f
        p.op("gpsimd", c_ident(ident_f))
        p.op("gpsimd", c_ident(ident_b))
        p.op("gpsimd", lambda e: e.memset(ones_f[:], 1.0))
        p.op("gpsimd", lambda e: e.memset(ones_b[:], 1.0))

        def c_strictlow(e):
            e.memset(Lx[:], 0.0)
            return e.affine_select(out=Lx[:], in_=Lx[:], compare_op=ALU.is_ge,
                                   fill=1.0, base=0, pattern=[[-1, 128]],
                                   channel_multiplier=1)
        p.op("gpsimd", c_strictlow)

        REG = {}

        def c_bcreg(e):
            g = e.register("bcreg")
            REG["guard"] = g
            REG["bc"] = g.__enter__()
            return e.reg_mov(REG["bc"], ECAP - 1)
        p.op("gpsimd", c_bcreg)
        p.op("gpsimd", lambda e: e.memset(ones_col[:], 1.0))
        p.op("gpsimd", lambda e: e.iota(out=thr_row[:], pattern=[[C, E]], base=C,
                                        channel_multiplier=0,
                                        allow_small_or_imprecise_dtypes=True))
        p.op("gpsimd", lambda e: e.iota(out=eoff32[:], pattern=[[C, E]], base=0,
                                        channel_multiplier=0,
                                        allow_small_or_imprecise_dtypes=True))
        p.op("gpsimd", lambda e: e.iota(out=sel_b[:], pattern=[[-8, NH]], base=0,
                                        channel_multiplier=1,
                                        allow_small_or_imprecise_dtypes=True))
        t_consts = p.op("gpsimd", lambda e: e.iota(out=sel2_b[0:NH, :], pattern=[[1, 128]],
                                                   base=0, channel_multiplier=-8,
                                                   allow_small_or_imprecise_dtypes=True),
                        inc="tP")

        p.wait("vector", t_consts)
        t_sel = p.vchain("vector", [
            lambda e: e.tensor_scalar(
                out=selt[:], in0=sel_b[:], scalar1=1.0, scalar2=0.0,
                op0=ALU.mult, op1=ALU.is_ge),
            lambda e: e.tensor_scalar(
                out=sel_b[:], in0=sel_b[:], scalar1=-1.0, scalar2=-7.0,
                op0=ALU.mult, op1=ALU.is_ge),
            lambda e: e.tensor_mul(out=sel_b[:], in0=sel_b[:], in1=selt[:]),
            lambda e: e.tensor_scalar(
                out=sel2t[:], in0=sel2_b[0:NH, :], scalar1=1.0, scalar2=0.0,
                op0=ALU.mult, op1=ALU.is_ge),
            lambda e: e.tensor_scalar(
                out=sel2_b[0:NH, :], in0=sel2_b[0:NH, :], scalar1=-1.0, scalar2=-7.0,
                op0=ALU.mult, op1=ALU.is_ge),
            lambda e: e.tensor_mul(
                out=sel2_b[0:NH, :], in0=sel2_b[0:NH, :], in1=sel2t[:]),
        ], "tV")
        p.wait("sync", t_sel)
        p.dma("sync", sel2_b[32:32 + NH, :], sel2_b[0:NH, :], "sP0A")
        t_sel2rep = p.dma("sync", sel2_b[64:64 + NH, :], sel2_b[0:NH, :], "sP0A")

        for h in range(NH):
            p.dma("sync", pb_sp[h * 44:(h + 1) * 44][None, :],
                  pos_bias[:, h][None, :], "sP0A", allow_slow_non_contiguous=True)
        t_pb_sp = p.tok("sP0A")

        bqkv_r = b_qkv.rearrange("(s hh r c) -> s (hh r) c", s=3, hh=16, r=8, c=8)
        bg_r = b_gate.rearrange("(hh r c) -> (hh r) c", hh=16, r=8, c=8)
        p.dma("sync", bias_q[:], bqkv_r[0], "sBIAS")
        p.dma("sync", bias_k[:], bqkv_r[1], "sBIAS")
        p.dma("sync", bias_v[:], bqkv_r[2], "sBIAS")
        p.dma("sync", bias_g[:], bg_r, "sBIAS")
        p.dma("gpsimd", b2_bf[:], b2, "sB2")
        t_bias = p.tok("sBIAS")
        p.wait("vector", t_bias)
        t_bias = [t_bias, p.op("vector", lambda e: e.tensor_scalar(
            out=bias_q[:], in0=bias_q[:], scalar1=QSCALE, scalar2=None,
            op0=ALU.mult), inc="tV")]

        with contextlib.ExitStack() as p0s:
            pb_flat = p0s.enter_context(sb("pb_flat", [1, NH * 44], F32))
            rw_nat = p0s.enter_context(sb("rw_nat", [E, D], F32))
            b1_nat = p0s.enter_context(sb("b1_nat", [E, FF], F32))
            bo_nat = p0s.enter_context(sb("bo_nat", [8, 128], F32))
            wstage = [p0s.enter_context(sb(f"wstage{i}", [128, D], BF16)) for i in range(2)]
            wperm = [p0s.enter_context(sb(f"wperm{i}", [128, 8, 128], BF16)) for i in range(2)]
            p_tr8 = p0s.enter_context(nc.psum_tensor("p_tr8", [128, E], F32))
            p_pb = p0s.enter_context(nc.psum_tensor("p_pb", [128, 512], F32))
            p_pb2 = p0s.enter_context(nc.psum_tensor("p_pb2", [128, 192], F32))

            p.wait("sync", t_pb_sp)
            p.dma("sync", pb_flat[:], pb_sp[None, :], "sP0B")
            p.dma("sync", rw_nat[:], router_w, "sP0B")
            p.dma("sync", b1_nat[:], b1, "sP0B")
            t_p0l = p.dma("sync", bo_nat[:], b_out.rearrange("(a b) -> a b", a=8), "sP0B")

            p.wait("tensor", t_p0l)
            p.op("tensor", lambda e: e.matmul(out=p_pb[:], lhsT=ones_f[:],
                                              rhs=pb_flat[:, 0:512], start=True, stop=True))
            t_pbmm = p.op("tensor", lambda e: e.matmul(out=p_pb2[:], lhsT=ones_f[:],
                                                       rhs=pb_flat[:, 512:704],
                                                       start=True, stop=True), inc="tT")
            p.wait("scalar", t_pbmm)
            pbf = pb_bc[:].rearrange("p h i -> p (h i)")
            p.op("scalar", lambda e: e.copy(out=pbf[:, 0:512], in_=p_pb[:]))
            t_pbdone = p.op("scalar", lambda e: e.copy(out=pbf[:, 512:704], in_=p_pb2[:]),
                            inc="tC")

            def small_tr(src_ap, dst_ap):
                kk = src_ap.shape[0]
                tt = p.op("tensor", lambda e, s=src_ap, kk=kk: e.transpose(
                    out=p_tr8[:, 0:E], in_=s, identity=ident_f[0:kk, 0:kk]), inc="tT")
                p.wait("scalar", tt)
                tc = p.op("scalar", lambda e, d=dst_ap: e.copy(out=d, in_=p_tr8[:, 0:E]),
                          inc="tC")
                p.wait("tensor", tc)

            for hc in range(8):
                small_tr(rw_nat[:, hc * 128:(hc + 1) * 128], rwT[:, hc, :])
            for fc in range(32):
                small_tr(b1_nat[:, fc * 128:(fc + 1) * 128], bias_b1[:, :, fc])
            small_tr(bo_nat[:, 0:128], bias_o[:, :])

            stage_free = [None, None]
            perm_free = [None, None]
            cnt = 0
            for sec in range(4):
                wdst = (wq_perm, wk_perm, wv_perm, wg_perm)[sec]
                for hc in range(8):
                    si = cnt % 2
                    src = (w_qkv[hc * 128:(hc + 1) * 128, sec * D:(sec + 1) * D]
                           if sec < 3 else w_gate[hc * 128:(hc + 1) * 128, :])
                    p.wait("gpsimd", stage_free[si])
                    t_stage = p.dma("gpsimd", wstage[si][:], src, "sWS0" if si == 0 else "sWS1")
                    p.wait("vector", [t_stage, perm_free[si]])
                    t_perm = p.op("vector", lambda e, si=si: e.tensor_copy(
                        out=wperm[si][:].rearrange("p c (hh r) -> p c hh r", hh=16, r=8),
                        in_=wstage[si][:].rearrange("p (hh r c) -> p c hh r", hh=16, r=8, c=8)),
                        inc="tV")
                    stage_free[si] = t_perm
                    p.wait("sync", t_perm)
                    perm_free[si] = p.dma("sync",
                                          wdst[:, hc, :, :].rearrange("c p m -> p c m"),
                                          wperm[si][:], "sWPERM")
                    cnt += 1
            t_wperm = p.tok("sWPERM")


        p.barrier()
        t_kv_done = None
        if "A1" in phases:
            with contextlib.ExitStack() as sA:
                x_tm = [sA.enter_context(sb(f"x_tm{i}", [128, D], F32)) for i in range(2)]
                sq_t = sA.enter_context(sb("sq_t", [128, D], F32))
                stat = [sA.enter_context(sb(f"stat{i}", [128, 8], F32)) for i in range(2)]
                xn_tm_b = [sA.enter_context(sb(f"xn_tm_b{i}", [128, D], BF16)) for i in range(2)]
                xn_tm_f = [sA.enter_context(sb(f"xn_tm_f{i}", [128, D], F32)) for i in range(2)]
                xnf_fm = [sA.enter_context(sb(f"xnf_fm{i}", [128, 8, 128], F32)) for i in range(2)]
                xn_fm = [sA.enter_context(sb(f"xn_fm{i}", [128, 8, NT], BF16)) for i in range(2)]
                logit_f = [sA.enter_context(sb(f"logit_f{i}", [E, 128], F32)) for i in range(2)]
                logit_tm = [sA.enter_context(sb(f"logit_tm{i}", [128, E], F32)) for i in range(2)]
                topk = [sA.enter_context(sb(f"topk{i}", [128, 8], F32)) for i in range(2)]
                eqm = [sA.enter_context(sb(f"eqm{i}", [128, E], F32)) for i in range(2)]
                eqm2 = [sA.enter_context(sb(f"eqm2{i}", [128, E], F32)) for i in range(2)]
                msum = [sA.enter_context(sb(f"msum{i}", [128, E], F32)) for i in range(2)]
                coef_tm = [sA.enter_context(sb(f"coef_tm{i}", [128, E], F32)) for i in range(2)]
                cstage = sA.enter_context(sb("cstage", [E, 128], F32)) if debug else None
                wkv_buf = [sA.enter_context(sb(f"wkv{i}", [128, 8, 128], BF16)) for i in range(2)]
                p_xnT = [sA.enter_context(nc.psum_tensor(f"p_xnT{i}", [128, 512], F32))
                         for i in range(2)]
                p_rt = sA.enter_context(nc.psum_tensor("p_rt", [E, 128], F32))
                p_rtT = sA.enter_context(nc.psum_tensor("p_rtT", [128, E], F32))
                p_cT = sA.enter_context(nc.psum_tensor("p_cT", [E, 128], F32))
                p_kv = [sA.enter_context(nc.psum_tensor(f"p_kv{i}", [128, NT], F32))
                        for i in range(2)]
                p_pos = sA.enter_context(nc.psum_tensor("p_pos", [128, 2 * E], F32))

                x_free = [None, None]
                xnb_free = [None, None]
                xnfm_free = [None, None]
                xnfm_ready = [None, None]
                wkv_free = [None, None]
                pxnT_free = [None, None]
                pkv_free = [None, None]
                xn_tm_f_free = [None, None]
                xnf_fm_free = [None, None]
                p_rt_free = None
                p_rtT_free = None
                p_cT_free = None
                msum_free = [None, None]
                lgt_free = [None, None]
                sq_free = [None]
                p_pos_free = None

                SS = {}

                def a1_h1(ss):
                    t, s = ss // SUBS, ss % SUBS
                    fmslot = t % 2
                    if True:
                        xslot = ss % 2
                        row0 = t * NT + s * 128
                        p.wait("sync", x_free[xslot])
                        tx = p.dma("sync", x_tm[xslot][:], x[row0:row0 + 128, :],
                                   "sX0" if xslot == 0 else "sX1")
                        xs = xslot
                        st_, sq_, xnf_, xff_ = stat[xs], sq_t, xn_tm_f[xs], xnf_fm[xs]
                        lgf_, lgt_, tk_ = logit_f[xs], logit_tm[xs], topk[xs]
                        eq_, eq2_, ms_, cf_ = eqm[xs], eqm2[xs], msum[xs], coef_tm[xs]
                        p.wait("scalar", [tx, sq_free[0]])
                        tsq = p.op("scalar", lambda e, xs=xs, sq_=sq_: e.activation(
                            out=sq_[:], in_=x_tm[xs][:], func=AF.Square), inc="tC")
                        p.wait("vector", tx)
                        p.op("vector", lambda e, xs=xs, st_=st_: e.reduce_sum(
                            out=st_[:, 0:1], in_=x_tm[xs][:], axis=AX.X))
                        p.wait("vector", tsq)
                        tred = p.op("vector", lambda e, st_=st_, sq_=sq_: e.reduce_sum(
                            out=st_[:, 1:2], in_=sq_[:], axis=AX.X), inc="tV")
                        sq_free[0] = tred
                        p.wait("vector", tred)
                        tvar = p.vchain("vector", [
                            lambda e, st_=st_: e.tensor_scalar(
                                out=st_[:, 2:3], in0=st_[:, 0:1], scalar1=1.0 / D,
                                scalar2=None, op0=ALU.mult),
                            lambda e, st_=st_: e.tensor_mul(
                                out=st_[:, 6:7], in0=st_[:, 2:3], in1=st_[:, 2:3]),
                            lambda e, st_=st_: e.scalar_tensor_tensor(
                                out=st_[:, 3:4], in0=st_[:, 1:2], scalar=1.0 / D,
                                in1=st_[:, 6:7], op0=ALU.mult, op1=ALU.subtract),
                            lambda e, st_=st_: e.tensor_scalar(
                                out=st_[:, 3:4], in0=st_[:, 3:4], scalar1=EPS,
                                scalar2=None, op0=ALU.add),
                        ], "tV")
                        p.wait("scalar", tvar)
                        trstd = p.vchain("scalar", [
                            lambda e, st_=st_: e.activation(
                                out=st_[:, 7:8], in_=st_[:, 3:4], func=AF.Ln),
                            lambda e, st_=st_: e.activation(
                                out=st_[:, 4:5], in_=st_[:, 7:8], func=AF.Exp,
                                scale=-0.5),
                        ], "tC")
                        p.wait("vector", [trstd, xnb_free[xslot], xn_tm_f_free[xs]])
                        rstd_bc = st_[:, 4:5].broadcast_to([128, D])
                        p.op("vector", lambda e, xs=xs, rb=rstd_bc, st_=st_: e.scalar_tensor_tensor(
                            out=xn_tm_b[xs][:], in0=x_tm[xs][:], scalar=st_[:, 2:3],
                            in1=rb, op0=ALU.subtract, op1=ALU.mult))
                        txn = p.op("vector", lambda e, xs=xs, rb=rstd_bc, st_=st_, xnf_=xnf_:
                                   e.scalar_tensor_tensor(
                            out=xnf_[:], in0=x_tm[xs][:], scalar=st_[:, 2:3],
                            in1=rb, op0=ALU.subtract, op1=ALU.mult), inc="tV")
                        x_free[xslot] = [txn, tred]
                        p.wait("sync", txn)
                        xnb_free[xslot] = p.dma("sync", xn_sp[row0:row0 + 128, :],
                                                xn_tm_b[xslot][:], "sXNSP")
                        if debug:
                            p.dma("sync", dbg["xn"][row0:row0 + 128, :], xnf_[:], "sDBG")
                        SS[ss] = dict(txn=txn)

                def a1_h2(ss):
                    nonlocal p_rt_free, p_rtT_free
                    t, s = ss // SUBS, ss % SUBS
                    fmslot = t % 2
                    if True:
                        xslot = ss % 2
                        row0 = t * NT + s * 128
                        xs = xslot
                        xnf_, xff_, lgf_, lgt_ = xn_tm_f[xs], xnf_fm[xs], logit_f[xs], logit_tm[xs]
                        txn = SS[ss]["txn"]
                        ttr_last = None
                        for g in range(2):
                            p.wait("tensor", [txn, pxnT_free[g]])
                            for hc4 in range(4):
                                hc = g * 4 + hc4
                                last = (hc4 == 3)
                                ttr = p.op("tensor", lambda e, hc=hc, hc4=hc4, g=g, xnf_=xnf_:
                                           e.transpose(
                                    out=p_xnT[g][:, hc4 * 128:(hc4 + 1) * 128],
                                    in_=xnf_[:, hc * 128:(hc + 1) * 128],
                                    identity=ident_f[:]), inc="tT" if last else None)
                            ttr_last = ttr
                            p.wait("scalar", ttr)
                            tec = p.op("scalar", lambda e, g=g, s=s, fs=fmslot: e.copy(
                                out=xn_fm[fs][:, 4 * g:4 * (g + 1), s * 128:(s + 1) * 128],
                                in_=p_xnT[g][:].rearrange("p (c n) -> p c n", c=4)), inc="tC")
                            p.wait("vector", [tec, xnf_fm_free[xs]])
                            tev = p.op("vector", lambda e, g=g, xff_=xff_: e.tensor_copy(
                                out=xff_[:, 4 * g:4 * (g + 1), :],
                                in_=p_xnT[g][:].rearrange("p (c n) -> p c n", c=4)), inc="tV")
                            pxnT_free[g] = [tec, tev]
                        xn_tm_f_free[xs] = [ttr_last] + ([p.tok("sDBG")] if debug else [])
                        t_xnfm_sub = p.tok("tC")
                        t_xnf = p.tok("tV")
                        p.wait("tensor", [t_xnf, t_sel, p_rt_free])
                        for hc in range(8):
                            trt = p.op("tensor", lambda e, hc=hc, xff_=xff_: e.matmul(
                                out=p_rt[:], lhsT=rwT[:, hc, :], rhs=xff_[:, hc, :],
                                start=(hc == 0), stop=(hc == 7)),
                                inc="tT" if hc == 7 else None)
                        xnf_fm_free[xs] = trt
                        p.wait("scalar", trt)
                        p_rt_free = p.op("scalar", lambda e, lgf_=lgf_: e.copy(
                            out=lgf_[:], in_=p_rt[:]), inc="tC")
                        p.wait("tensor", [p_rt_free, p_rtT_free])
                        trtT = p.op("tensor", lambda e, lgf_=lgf_: e.transpose(
                            out=p_rtT[:], in_=lgf_[:], identity=ident_f[0:E, 0:E]), inc="tT")
                        p.wait("vector", [trtT, lgt_free[xs]])
                        p_rtT_free = p.op("vector", lambda e, lgt_=lgt_: e.tensor_copy(
                            out=lgt_[:], in_=p_rtT[:]), inc="tV")
                        tl = p_rtT_free
                        lgt_free[xs] = None
                        SS[ss]["tl"] = tl

                def a1_h3(ss):
                    nonlocal p_cT_free, p_pos_free
                    t, s = ss // SUBS, ss % SUBS
                    if True:
                        xslot = ss % 2
                        row0 = t * NT + s * 128
                        xs = xslot
                        lgt_, tk_ = logit_tm[xs], topk[xs]
                        eq_, eq2_, ms_, cf_ = eqm[xs], eqm2[xs], msum[xs], coef_tm[xs]
                        tl = SS.pop(ss)["tl"]
                        p.wait("vector", tl)
                        tdiff = p.vchain("vector", [
                            lambda e, tk_=tk_, lgt_=lgt_: e.reduce_max(
                                out=tk_[:, 0:1], in_=lgt_[:], axis=AX.X),
                            lambda e, tk_=tk_, lgt_=lgt_, eq_=eq_: e.tensor_scalar(
                                out=eq_[:], in0=lgt_[:], scalar1=tk_[:, 0:1],
                                scalar2=None, op0=ALU.is_ge),
                            lambda e, eq_=eq_, eq2_=eq2_, lgt_=lgt_: e.scalar_tensor_tensor(
                                out=eq2_[:], in0=eq_[:], scalar=-1e30, in1=lgt_[:],
                                op0=ALU.mult, op1=ALU.add),
                            lambda e, tk_=tk_, eq2_=eq2_: e.reduce_max(
                                out=tk_[:, 1:2], in_=eq2_[:], axis=AX.X),
                            lambda e, tk_=tk_: e.tensor_sub(
                                out=tk_[:, 6:7], in0=tk_[:, 1:2], in1=tk_[:, 0:1]),
                        ], "tV")
                        lgt_free[xs] = tdiff
                        p.wait("scalar", tdiff)
                        tw2r = p.op("scalar", lambda e, tk_=tk_: e.activation(
                            out=tk_[:, 2:3], in_=tk_[:, 6:7], func=AF.Exp), inc="tC")
                        ti = t * SUBS + s
                        p.wait("vector", [tw2r, msum_free[xs]])
                        tcoef = p.vchain("vector", [
                            lambda e, tk_=tk_: e.tensor_scalar(
                                out=tk_[:, 3:4], in0=tk_[:, 2:3], scalar1=1.0,
                                scalar2=None, op0=ALU.add),
                            lambda e, tk_=tk_: e.reciprocal(
                                out=tk_[:, 4:5], in_=tk_[:, 3:4]),
                            lambda e, tk_=tk_: e.tensor_mul(
                                out=tk_[:, 5:6], in0=tk_[:, 2:3], in1=tk_[:, 4:5]),
                            lambda e, tk_=tk_, eq2_=eq2_: e.tensor_scalar(
                                out=eq2_[:], in0=eq2_[:], scalar1=tk_[:, 1:2],
                                scalar2=None, op0=ALU.is_ge),
                            lambda e, ti=ti, eq2_=eq2_: e.tensor_copy(
                                out=mask2_all[:, ti, :], in_=eq2_[:]),
                            lambda e, ms_=ms_, eq_=eq_, eq2_=eq2_: e.tensor_add(
                                out=ms_[:], in0=eq_[:], in1=eq2_[:]),
                            lambda e, tk_=tk_, eq2_=eq2_: e.tensor_scalar(
                                out=eq2_[:], in0=eq2_[:], scalar1=tk_[:, 5:6],
                                scalar2=None, op0=ALU.mult),
                            lambda e, cf_=cf_, eq_=eq_, tk_=tk_, eq2_=eq2_: e.scalar_tensor_tensor(
                                out=cf_[:], in0=eq_[:], scalar=tk_[:, 4:5],
                                in1=eq2_[:], op0=ALU.mult, op1=ALU.add),
                            lambda e, ti=ti, eq_=eq_: e.tensor_copy(
                                out=mask1_all[:, ti, :], in_=eq_[:]),
                            lambda e, ti=ti, tk_=tk_: e.tensor_copy(
                                out=wab[:, ti, :], in_=tk_[:, 4:6]),
                        ], "tV")
                        p.wait("tensor", [tcoef, t_consts, p_pos_free])
                        p.op("tensor", lambda e, ms_=ms_: e.matmul(
                            out=p_pos[:, 0:E], lhsT=Lx[:], rhs=ms_[:],
                            start=True, stop=True))
                        tposm = p.op("tensor", lambda e, ms_=ms_: e.matmul(
                            out=p_pos[0:1, E:2 * E], lhsT=ones_col[:], rhs=ms_[:],
                            start=True, stop=True), inc="tT")
                        msum_free[xs] = tposm
                        p.wait("scalar", tposm)
                        p.op("scalar", lambda e, ti=ti: e.copy(
                            out=pos_all[:, ti, :], in_=p_pos[:, 0:E]))
                        tpose = p.op("scalar", lambda e, ti=ti: e.copy(
                            out=cnt_nat[:, ti, :], in_=p_pos[0:1, E:2 * E]), inc="tC")
                        p_pos_free = tpose
                        p.wait("tensor", [tcoef, p_cT_free])
                        tcT = p.op("tensor", lambda e, cf_=cf_: e.transpose(
                            out=p_cT[:], in_=cf_[:], identity=ident_f[:]), inc="tT")
                        p.wait("scalar", tcT)
                        tcbf = p.op("scalar", lambda e, r0=row0: e.copy(
                            out=coef_bf[:, r0:r0 + 128], in_=p_cT[:]), inc="tC")
                        p_cT_free = [tcbf]
                        if debug:
                            tdc = p.op("scalar", lambda e: e.copy(
                                out=cstage[:], in_=p_cT[:]), inc="tC")
                            p_cT_free.append(tdc)
                            p.wait("sync", tdc)
                            tdcd = p.dma("sync", dbg["coef"][:, row0:row0 + 128],
                                         cstage[:], "sDBG")
                def emit_kv(t):
                    fmslot = t % 2
                    xnfm_ready[fmslot] = p.tok("tC")
                    p.wait("sync", t_wperm)
                    tkv_mm = None
                    for kv in range(2):
                        wsrc = (wk_perm, wv_perm)[kv]
                        dst = (k_fm, v_fm)[kv]
                        bias = (bias_k, bias_v)[kv]
                        for mc in range(8):
                            i = kv * 8 + mc
                            wslot = i % 2
                            pslot = i % 2
                            p.wait("sync", wkv_free[wslot])
                            twl = p.dma("sync", wkv_buf[wslot][:],
                                        wsrc[mc].rearrange("h p m -> p h m"),
                                        "sWKV0" if wslot == 0 else "sWKV1")
                            p.wait("tensor", [twl, xnfm_ready[fmslot], pkv_free[pslot]])
                            for hc in range(8):
                                tkv_mm = p.op(
                                    "tensor",
                                    lambda e, hc=hc, ws=wslot, ps=pslot, fs=fmslot:
                                    e.matmul(out=p_kv[ps][:],
                                             lhsT=wkv_buf[ws][:, hc, :],
                                             rhs=xn_fm[fs][:, hc, :],
                                             start=(hc == 0), stop=(hc == 7)),
                                    inc="tT" if hc == 7 else None)
                            wkv_free[wslot] = tkv_mm
                            p.wait("vector", tkv_mm)
                            pkv_free[pslot] = p.op(
                                "vector",
                                lambda e, d=dst, mc=mc, t=t, b=bias, ps=pslot:
                                e.tensor_scalar(out=d[:, mc, t * NT:(t + 1) * NT],
                                                in0=p_kv[ps][:], scalar1=b[:, mc:mc + 1],
                                                scalar2=None, op0=ALU.add), inc="tV")
                    xnfm_free[fmslot] = tkv_mm

                for ss in range(NSUB + 2):
                    if ss < NSUB:
                        a1_h1(ss)
                    if 1 <= ss <= NSUB:
                        a1_h2(ss - 1)
                        if (ss - 1) % SUBS == SUBS - 1:
                            emit_kv((ss - 1) // SUBS)
                    if ss >= 2:
                        a1_h3(ss - 2)

                t_kv_done = [p.tok("tC"), p.tok("tV")]
                if debug:
                    p.wait("gpsimd", t_kv_done)
                    for c in range(8):
                        p.dma("gpsimd", dbg["k"][c], k_fm[:, c, :], "sDBG2")
                        p.dma("gpsimd", dbg["v"][c], v_fm[:, c, :], "sDBG2")

        p.barrier()
        if "IDX" in phases:
            with contextlib.ExitStack() as sI:
                cnt32 = sI.enter_context(sb("cnt32", [32, E], F32))
                base32 = sI.enter_context(sb("base32", [32, E], F32))
                base_row = sI.enter_context(sb("base_row", [1, NSUB * E], F32))
                gtmp = sI.enter_context(sb("gtmp", [128, E], F32))
                ovf = sI.enter_context(sb("ovf", [128, E], F32))
                mr = sI.enter_context(sb("mr", [128, E], F32))
                p_idx = sI.enter_context(nc.psum_tensor("p_idx", [32, E], F32))
                p_bb = sI.enter_context(nc.psum_tensor("p_bb", [128, NSUB * E], F32))

                t_eoff = t_consts
                p.dma("sync", cnt_dram.rearrange("a b -> (a b)")[None, :],
                      cnt_nat[:].rearrange("p a b -> p (a b)"), "sIDX")
                p.wait("sync", p.tok("sIDX"))
                t_c32 = p.dma("sync", cnt32[:], cnt_dram, "sIDX")
                p.wait("tensor", t_c32)
                t_pfx = p.op("tensor", lambda e: e.matmul(
                    out=p_idx[:], lhsT=Lx[0:32, 0:32], rhs=cnt32[:],
                    start=True, stop=True), inc="tT")
                p.wait("scalar", t_pfx)
                t_b32 = p.op("scalar", lambda e: e.copy(out=base32[:], in_=p_idx[:]),
                             inc="tC")
                p.wait("vector", [t_b32, t_eoff])
                t_badd = p.op("vector", lambda e: e.tensor_add(
                    out=base32[:], in0=base32[:], in1=eoff32[:]), inc="tV")
                p.wait("sync", t_badd)
                p.dma("sync", base_dram, base32[:], "sIDX")
                p.wait("sync", p.tok("sIDX"))
                t_brow = p.dma("sync", base_row[:],
                               base_dram.rearrange("a b -> (a b)")[None, :], "sIDX")
                p.wait("tensor", t_brow)
                t_bbm = p.op("tensor", lambda e: e.matmul(
                    out=p_bb[:], lhsT=ones_f[:], rhs=base_row[:],
                    start=True, stop=True), inc="tT")
                p.wait("scalar", t_bbm)
                t_bbc = p.op("scalar", lambda e: e.copy(
                    out=base_bc[:].rearrange("p a b -> p (a b)"), in_=p_bb[:]),
                    inc="tC")
                p.wait("vector", t_bbc)
                for ti in range(NSUB):
                    p.vchain("vector", [
                        lambda e, ti=ti: e.tensor_add(
                            out=gtmp[:], in0=base_bc[:, ti, :], in1=pos_all[:, ti, :]),
                        lambda e: e.tensor_tensor(
                            out=ovf[:], in0=gtmp[:], in1=thr_row[:], op=ALU.is_ge),
                        lambda e: e.scalar_tensor_tensor(
                            out=gtmp[:], in0=ovf[:], scalar=BIG, in1=gtmp[:],
                            op0=ALU.mult, op1=ALU.add),
                        lambda e, ti=ti: e.tensor_mul(
                            out=mr[:], in0=mask1_all[:, ti, :], in1=gtmp[:]),
                        lambda e, ti=ti: e.reduce_sum(
                            out=idx1f[:, ti:ti + 1], in_=mr[:], axis=AX.X),
                        lambda e, ti=ti: e.tensor_mul(
                            out=mr[:], in0=mask2_all[:, ti, :], in1=gtmp[:]),
                        lambda e, ti=ti: e.reduce_sum(
                            out=idx2f[:, ti:ti + 1], in_=mr[:], axis=AX.X),
                    ], "tV")
                t_idxc = p.vchain("vector", [
                    lambda e: e.tensor_copy(out=idx1u[:], in_=idx1f[:]),
                    lambda e: e.tensor_copy(out=idx2u[:], in_=idx2f[:]),
                ], "tV")

            t_xnsp_all = p.tok("sXNSP")
            t_idx_ready = t_idxc
            xnsc_free = [None]
            p.barrier()
        route_stack.close()

        if "A2a" in phases:
            with contextlib.ExitStack() as sB:
                xn2 = sB.enter_context(sb("xn2", [128, 8, NT], BF16))
                wqg_buf = [sB.enter_context(sb(f"wqg{i}", [128, 8, 128], BF16)) for i in range(2)]
                q_fm = sB.enter_context(sb("q_fm", [128, 8, NT], BF16))
                gate_fm = sB.enter_context(sb("gate_fm", [128, 8, NT], BF16))
                prod_sb = [sB.enter_context(sb(f"prod{i}", [128, NT], BF16)) for i in range(2)]
                qk_stage = [sB.enter_context(sb(f"qkst{i}", [80, NT], BF16)) for i in range(2)]
                qkT_sb = sB.enter_context(sb("qkT", [128, SUBS, 8, 80], BF16))
                scores_tm = sB.enter_context(sb("scores_tm", [128, NH, 44], F32))
                sm_stat = sB.enter_context(sb("sm_stat", [128, NH, 2], F32))
                attn_coll = sB.enter_context(sb("attn_coll", [128, 24, NH], BF16))
                attnT = sB.enter_context(sb("attnT", [80, 8, NT], BF16))
                bcast_sb = [sB.enter_context(sb(f"bcs{i}", [128, NT], BF16)) for i in range(2)]
                avtmp = sB.enter_context(sb("avtmp", [128, NT], BF16))
                avacc = sB.enter_context(sb("avacc", [128, 8, NT], BF16))
                p_bc = [sB.enter_context(nc.psum_tensor(f"p_bc{i}", [128, NT], F32)) for i in range(2)]
                p_sc = [sB.enter_context(nc.psum_tensor(f"p_sc{i}", [NH, NT], F32)) for i in range(2)]
                p_qkT = [sB.enter_context(nc.psum_tensor(f"p_qkT{i}", [128, 80], BF16)) for i in range(2)]
                p_at = [sB.enter_context(nc.psum_tensor(f"p_at{i}", [NH, 128], BF16)) for i in range(2)]

                t_spills = p.tok("sXNSP")
                wqg_free = [None, None]
                pbc_free = [None, None]
                psc_free = [None, None]
                pqkT_free = [None, None]
                pat_free = [None, None]
                bcs_free = [None, None]
                prod_free = [None, None]
                qkst_free = [None, None]
                xn2_free = None
                qg_evac_last = None
                avacc_free = None
                attnT_free = None

                for t in range(NTILES):
                    c0 = t * NT
                    p.wait("sync", [t_spills, xn2_free])
                    for hc in range(8):
                        txn2 = p.dma("sync", xn2[:, hc, :],
                                     xn_sp[c0:c0 + NT, hc * 128:(hc + 1) * 128],
                                     "sXNL0", transpose=True)
                    for sec in range(2):
                        wsrc = (wq_perm, wg_perm)[sec]
                        dst = (q_fm, gate_fm)[sec]
                        bias = (bias_q, bias_g)[sec]
                        for mc in range(8):
                            i = sec * 8 + mc
                            ws = i % 2
                            pslot = i % 2
                            p.wait("sync", wqg_free[ws])
                            twl = p.dma("sync", wqg_buf[ws][:],
                                        wsrc[mc].rearrange("h p m -> p h m"),
                                        "sWKV0" if ws == 0 else "sWKV1")
                            p.wait("tensor", [twl, txn2, pbc_free[pslot], t_bias])
                            for hc in range(8):
                                tqg = p.op("tensor",
                                           lambda e, hc=hc, ws=ws, ps=pslot:
                                           e.matmul(out=p_bc[ps][:],
                                                    lhsT=wqg_buf[ws][:, hc, :],
                                                    rhs=xn2[:, hc, :],
                                                    start=(hc == 0), stop=(hc == 7)),
                                           inc="tT" if hc == 7 else None)
                            wqg_free[ws] = tqg
                            p.wait("scalar", tqg)
                            if sec == 0:
                                qg_evac_last = p.op(
                                    "scalar", lambda e, mc=mc, ps=pslot: e.activation(
                                        out=q_fm[:, mc, :], in_=p_bc[ps][:],
                                        func=AF.Identity, bias=bias_q[:, mc:mc + 1],
                                        scale=QSCALE), inc="tC")
                            else:
                                qg_evac_last = p.op(
                                    "scalar", lambda e, mc=mc, ps=pslot: e.activation(
                                        out=gate_fm[:, mc, :], in_=p_bc[ps][:],
                                        func=AF.Sigmoid, bias=bias_g[:, mc:mc + 1]),
                                    inc="tC")
                            pbc_free[pslot] = qg_evac_last
                    xn2_free = tqg
                    tq_done = p.tok("tC")
                    if debug and t == 0:
                        p.wait("gpsimd", tq_done)
                        for c in range(8):
                            p.dma("gpsimd", dbg["q"][c], q_fm[:, c, :], "sDBG2")

                    for oi, o in enumerate(DIST):
                        g, ol = oi // 3, oi % 3
                        v0 = max(0, min(NT, o - c0))
                        pslot = oi % 2
                        sslot = g % 2
                        if v0 < NT:
                            v0p = min(NT, v0 + (v0 & 1))
                            for c in range(8):
                                pr = c % 2
                                p.wait("vector", [tq_done, prod_free[pr]])
                                if v0 > 0:
                                    p.op("vector", lambda e, pr=pr, v0p=v0p: e.memset(
                                        prod_sb[pr][:, 0:v0p], 0.0))
                                tpr_c = p.op("vector", lambda e, c=c, pr=pr, v0=v0, o=o, c0=c0:
                                             e.tensor_mul(
                                                 out=prod_sb[pr][:, v0:],
                                                 in0=q_fm[:, c, v0:],
                                                 in1=k_fm[:, c, c0 - o + v0:c0 + NT - o]),
                                             inc="tV")
                                p.wait("tensor", [tpr_c, psc_free[pslot]] if c == 0 else tpr_c)
                                tmm = p.op("tensor", lambda e, c=c, pr=pr, ps=pslot:
                                           e.matmul(out=p_sc[ps][:],
                                                    lhsT=sel_b[:],
                                                    rhs=prod_sb[pr][:],
                                                    start=(c == 0), stop=(c == 7)),
                                           inc="tT")
                                prod_free[pr] = tmm
                            tsc = tmm
                        else:
                            tsc = None
                        tze = None
                        if v0 > 0:
                            v0p = v0 + (v0 & 1)
                            p.wait("vector", qkst_free[sslot] if ol == 0 else None)
                            tze = p.op("vector", lambda e, ol=ol, ss=sslot, v0p=v0p:
                                       e.memset(qk_stage[ss][ol * 32:ol * 32 + NH, 0:v0p],
                                                0.0), inc="tV")
                        if v0 < NT:
                            p.wait("scalar", [tsc, tze,
                                              qkst_free[sslot] if ol == 0 else None])
                            tse = p.op("scalar", lambda e, ps=pslot, ol=ol, ss=sslot, v0=v0:
                                       e.activation(
                                           out=qk_stage[ss][ol * 32:ol * 32 + NH, v0:],
                                           in_=p_sc[ps][:, v0:],
                                           func=AF.Identity), inc="tC")
                            psc_free[pslot] = tse
                            tse = [tse, tze]
                        else:
                            tse = tze
                        if debug and t == 0 and oi == 5:
                            p.wait("gpsimd", tse)
                            tqksd = p.dma("gpsimd", dbg["qks"][:], qk_stage[sslot][:], "sDBG2")
                            p.wait("tensor", tqksd)
                        if ol == 2 or oi == len(DIST) - 1:
                            for su in range(SUBS):
                                pq = (g * SUBS + su) % 2
                                p.wait("tensor", [tse, pqkT_free[pq]])
                                ttr = p.op("tensor", lambda e, ss=sslot, su=su, pq=pq:
                                           e.transpose(out=p_qkT[pq][:],
                                                       in_=qk_stage[ss][:, su * 128:(su + 1) * 128],
                                                       identity=ident_b[0:80, 0:80]),
                                           inc="tT")
                                p.wait("vector", ttr)
                                tqkc = p.op("vector", lambda e, g=g, su=su, pq=pq:
                                            e.tensor_copy(out=qkT_sb[:, su, g, :],
                                                          in_=p_qkT[pq][:]), inc="tV")
                                pqkT_free[pq] = tqkc
                            qkst_free[sslot] = p.tok("tV")
                    t_qkT = p.tok("tV")
                    if debug and t == 0:
                        p.wait("gpsimd", t_qkT)
                        tqkd = p.dma("gpsimd", dbg["qkt"][:].rearrange("p s g w -> p (s g w)"),
                                     qkT_sb[:].rearrange("p s g w -> p (s g w)"), "sDBG2")
                        p.wait("vector", tqkd)

                    for su in range(SUBS):
                        p.wait("vector", t_qkT)
                        for i, off in enumerate(OFFSETS):
                            di = DIST_IDX[off]
                            g, ol = di // 3, di % 3
                            p.op("vector", lambda e, i=i, g=g, ol=ol, su=su: e.tensor_add(
                                out=scores_tm[:, :, i],
                                in0=qkT_sb[:, su, g, ol * 32:ol * 32 + NH],
                                in1=pb_bc[:, :, i]))
                        if debug and t == 0 and su == 0:
                            texpn = p.op("vector", lambda e: e.engine_nop(), inc="tV")
                            p.wait("sync", texpn)
                            tscd = p.dma("sync", dbg["sc"][:],
                                  scores_tm[:].rearrange("p h i -> p (h i)"), "sDBG")
                            p.wait("vector", tscd)
                        p.op("vector", lambda e: e.reduce_max(
                            out=sm_stat[:, :, 0], in_=scores_tm[:], axis=AX.X), inc="tV")
                        p.wait("vector", p.tok("tV"))
                        p.op("vector", lambda e: e.tensor_sub(
                            out=scores_tm[:], in0=scores_tm[:],
                            in1=sm_stat[:, :, 0:1].broadcast_to([128, NH, 44])), inc="tV")
                        texp_in = p.tok("tV")
                        p.wait("scalar", texp_in)
                        texp = p.op("scalar", lambda e: e.activation(
                            out=scores_tm[:].rearrange("p h i -> p (h i)"),
                            in_=scores_tm[:].rearrange("p h i -> p (h i)"),
                            func=AF.Exp), inc="tC")
                        p.wait("vector", texp)
                        p.op("vector", lambda e: e.reduce_sum(
                            out=sm_stat[:, :, 1], in_=scores_tm[:], axis=AX.X), inc="tV")
                        p.wait("vector", p.tok("tV"))
                        p.op("vector", lambda e: e.reciprocal(
                            out=sm_stat[:, :, 1], in_=sm_stat[:, :, 1]), inc="tV")
                        p.wait("vector", p.tok("tV"))
                        p.op("vector", lambda e: e.tensor_mul(
                            out=scores_tm[:], in0=scores_tm[:],
                            in1=sm_stat[:, :, 1:2].broadcast_to([128, NH, 44])), inc="tV")
                        p.wait("vector", p.tok("tV"))
                        def _red0(e):
                            with nc.allow_low_precision(reason="attn collapse bf16 ok"):
                                return e.reduce_sum(out=attn_coll[:, 0, :],
                                                    in_=scores_tm[:, :, 0::4], axis=AX.X)
                        p.op("vector", _red0)
                        p.op("vector", lambda e: e.tensor_copy(
                            out=attn_coll[:, 1, :], in_=scores_tm[:, :, 1]))
                        p.op("vector", lambda e: e.tensor_add(
                            out=attn_coll[:, 2:22:2, :],
                            in0=scores_tm[:, :, 5::4].transpose([0, 2, 1]),
                            in1=scores_tm[:, :, 2:40:4].transpose([0, 2, 1])))
                        p.op("vector", lambda e: e.tensor_copy(
                            out=attn_coll[:, 22, :], in_=scores_tm[:, :, 42]))
                        tcl = p.op("vector", lambda e: e.tensor_copy(
                            out=attn_coll[:, 3:24:2, :],
                            in_=scores_tm[:, :, 3::4].transpose([0, 2, 1])), inc="tV")
                        p.wait("tensor", [tcl, attnT_free])
                        for oi in range(len(DIST)):
                            g, ol = oi // 3, oi % 3
                            pq = oi % 2
                            p.wait("tensor", pat_free[pq])
                            tat = p.op("tensor", lambda e, oi=oi, pq=pq: e.transpose(
                                out=p_at[pq][:], in_=attn_coll[:, oi, :],
                                identity=ident_b[:]), inc="tT")
                            p.wait("scalar", tat)
                            pat_free[pq] = p.op("scalar", lambda e, g=g, ol=ol, su=su, pq=pq:
                                                e.copy(out=attnT[ol * 32:ol * 32 + NH, g,
                                                                 su * 128:(su + 1) * 128],
                                                       in_=p_at[pq][:]), inc="tC")
                    t_at = p.tok("tC")

                    p.wait("vector", avacc_free)
                    first = True
                    for oi, o in enumerate(DIST):
                        v0 = max(0, min(NT, o - c0))
                        if v0 >= NT:
                            continue
                        g, ol = oi // 3, oi % 3
                        par = oi % 2
                        p.wait("tensor", [t_at, pbc_free[par]])
                        tb = p.op("tensor", lambda e, g=g, ol=ol, par=par: e.matmul(
                            out=p_bc[par][:], lhsT=sel2_b[ol * 32:ol * 32 + NH, :],
                            rhs=attnT[ol * 32:ol * 32 + NH, g, :],
                            start=True, stop=True), inc="tT")
                        p.wait("vector", [tb, bcs_free[par]])
                        tbe = p.op("vector", lambda e, par=par: e.tensor_copy(
                            out=bcast_sb[par][:], in_=p_bc[par][:]), inc="tV")
                        pbc_free[par] = tbe
                        for c in range(8):
                            if first:
                                tav = p.op("vector", lambda e, c=c, par=par, c0=c0:
                                           e.tensor_mul(out=avacc[:, c, :],
                                                        in0=bcast_sb[par][:],
                                                        in1=v_fm[:, c, c0:c0 + NT]),
                                           inc="tV" if c == 7 else None)
                            else:
                                p.op("vector", lambda e, c=c, par=par, v0=v0, o=o, c0=c0:
                                     e.tensor_mul(out=avtmp[:, v0:],
                                                  in0=bcast_sb[par][:, v0:],
                                                  in1=v_fm[:, c, c0 - o + v0:c0 + NT - o]))
                                tav = p.op("vector", lambda e, c=c, v0=v0:
                                           e.tensor_add(out=avacc[:, c, v0:],
                                                        in0=avacc[:, c, v0:],
                                                        in1=avtmp[:, v0:]),
                                           inc="tV" if c == 7 else None)
                        bcs_free[par] = tav
                        first = False
                    if debug and t == 0:
                        p.wait("gpsimd", [p.tok("tV")])
                        for c in range(8):
                            p.dma("gpsimd", dbg["av"][c], avacc[:, c, :], "sDBG2")
                        t_avd = p.tok("sDBG2")
                        p.wait("vector", t_avd)
                    for c in range(8):
                        tgv = p.op("vector", lambda e, c=c: e.tensor_mul(
                            out=avacc[:, c, :], in0=avacc[:, c, :],
                            in1=gate_fm[:, c, :]), inc="tV" if c == 7 else None)
                    p.wait("sync", tgv)
                    tspill = p.dma("sync",
                                   gav_sp[:, :, c0:c0 + NT].rearrange("c p n -> p c n"),
                                   avacc[:], "sGAV")
                    avacc_free = tspill
                    attnT_free = p.tok("tT")
                    for sub in range(t * SUBS, (t + 1) * SUBS):
                        par = 0
                        p.wait("sync", [t_xnsp_all, xnsc_free[par]])
                        tld = p.dma("sync", xn_sc[par][:],
                                    xn_sp[sub * 128:(sub + 1) * 128, :], "sXC")
                        p.wait("gpsimd", [tld, t_idx_ready])
                        p.op("gpsimd", lambda e, par=par, ti=sub: e.indirect_dma_start(
                            out=expert_x,
                            out_offset=IndirectOffsetOnAxis(ap=idx1u[:, ti:ti + 1], axis=0),
                            in_=xn_sc[par][:], in_offset=None,
                            bounds_check=REG["bc"], oob_is_err=False), inc="sSC", inck=16)
                        ts2 = p.op("gpsimd", lambda e, par=par, ti=sub: e.indirect_dma_start(
                            out=expert_x,
                            out_offset=IndirectOffsetOnAxis(ap=idx2u[:, ti:ti + 1], axis=0),
                            in_=xn_sc[par][:], in_offset=None,
                            bounds_check=REG["bc"], oob_is_err=False), inc="sSC", inck=16)
                        xnsc_free[par] = ts2
                    p.dma("gpsimd", w1_bf[t], w1[t], "qW")
                    p.dma("gpsimd", w2_bf[t], w2[t], "qW")
                if debug:
                    p.wait("gpsimd", [p.tok("sGAV")])
                    for c in range(8):
                        p.dma("gpsimd", dbg["gav"][c], gav_sp[c], "sDBG2")

        kv_stack.close()
        p.barrier()
        if "B1" in phases:
            with contextlib.ExitStack() as sC:
                acc = sC.enter_context(sb("acc", [128, 8, HALF], BF16))
                gav_t = [sC.enter_context(sb(f"gav_t{i}", [128, 8, NT], BF16)) for i in range(2)]
                w_out_sb = sC.enter_context(sb("w_out_sb", [128, 8, D], BF16))
                p_m2 = [sC.enter_context(nc.psum_tensor(f"p_m2{i}", [128, NT], F32)) for i in range(2)]

                wo_r = w_out.rearrange("(hh r c) m -> c (hh r) m", hh=16, r=8, c=8)
                for hc in range(8):
                    t_wo = p.dma("gpsimd", w_out_sb[:, hc, :], wo_r[hc], "sWO")

                pm2_free = [None, None]
                gav_free = [None, None]
                acc_free = None
                for hf in range(2):
                    H0 = hf * HALF
                    for nt in range(4):
                        gsl = nt % 2
                        p.wait("sync", [p.tok("sGAV"), gav_free[gsl]])
                        tgv = p.dma("sync", gav_t[gsl][:],
                                    gav_sp[:, :, H0 + nt * NT:H0 + (nt + 1) * NT]
                                    .rearrange("c p n -> p c n"),
                                    "sGV0" if gsl == 0 else "sGV1")
                        for mc in range(8):
                            par = (nt * 8 + mc) % 2
                            p.wait("tensor", [tgv, t_wo, pm2_free[par], t_bias])
                            for hc in range(8):
                                p.op("tensor", lambda e, hc=hc, mc=mc, gsl=gsl, par=par:
                                     e.matmul(out=p_m2[par][:],
                                              lhsT=w_out_sb[:, hc, mc * 128:(mc + 1) * 128],
                                              rhs=gav_t[gsl][:, hc, :],
                                              start=(hc == 0), stop=False), inc=None)
                            tai = p.op("tensor", lambda e, mc=mc, nt=nt, H0=H0, par=par:
                                       e.matmul(out=p_m2[par][:],
                                                lhsT=b2_bf[:, mc * 128:(mc + 1) * 128],
                                                rhs=coef_bf[:, H0 + nt * NT:H0 + (nt + 1) * NT],
                                                start=False, stop=True), inc="tT")
                            gav_free[gsl] = tai
                            p.wait("scalar", [tai, acc_free])
                            pm2_free[par] = p.op(
                                "scalar", lambda e, mc=mc, nt=nt, par=par: e.activation(
                                    out=acc[:, mc, nt * NT:(nt + 1) * NT],
                                    in_=p_m2[par][:], func=AF.Identity,
                                    bias=bias_o[:, mc:mc + 1]), inc="tC")
                    t_accinit = p.tok("tC")
                    p.wait("sync", t_accinit)
                    t_as = p.dma("sync", attn_sp[:, :, H0:H0 + HALF]
                                 .rearrange("c p n -> p c n"), acc[:], "sAS")
                    acc_free = t_as
            p.barrier()

        if "B2" in phases:
            with contextlib.ExitStack() as sE:
                xe = [sE.enter_context(sb(f"xe{i}", [128, 8, C], BF16)) for i in range(2)]
                w1g = [sE.enter_context(sb(f"w1g{i}", [128, 8, 1024], BF16)) for i in range(2)]
                w2g = sE.enter_context(sb("w2g", [128, 8, 1024], BF16))
                h1_sb = sE.enter_context(sb("h1_sb", [128, 8, NT], BF16))
                m2acc = sE.enter_context(sb("m2acc", [128, 8, C], BF16))
                ye_st = [sE.enter_context(sb(f"ye_st{i}", [128, D], BF16)) for i in range(2)]
                p_h1 = [sE.enter_context(nc.psum_tensor(f"p_h1{i}", [128, NT], F32)) for i in range(2)]
                p_m2e = [sE.enter_context(nc.psum_tensor(f"p_m2e{i}", [128, NT], F32)) for i in range(2)]
                p_ye = [sE.enter_context(nc.psum_tensor(f"p_ye{i}", [128, D], BF16)) for i in range(2)]

                t_wcast_all = p.tok("qW")
                t_scat_all = p.tok("sSC")
                xe_free = [None, None]
                w1_free = [None, None]
                w2_free = None
                ph1_free = [None, None]
                pm2e_free = [None, None]
                pye_free = [None, None]
                yest_free = [None, None]
                h1_free = None
                m2acc_free = None
                macc_tok = [[None] * CT for _ in range(8)]

                for ei in range(E):
                    par = ei % 2
                    p.wait("sync", [t_scat_all, xe_free[par]])
                    for hc in range(8):
                        t_xe = p.dma("sync", xe[par][:, hc, :],
                                     expert_x[ei * C:(ei + 1) * C,
                                              hc * 128:(hc + 1) * 128],
                                     "sEX0" if par == 0 else "sEX1", transpose=True)
                    for g in range(4):
                        wsl = g % 2
                        p.wait("sync", [("qW", 32 * (ei + 1)), w1_free[wsl]])
                        for hc in range(8):
                            t_w1 = p.dma("sync", w1g[wsl][:, hc, :],
                                         w1_bf[ei, g * 1024:(g + 1) * 1024,
                                               hc * 128:(hc + 1) * 128],
                                         "sWG0" if wsl == 0 else "sWG1",
                                         transpose=True)
                        p.wait("sync", w2_free)
                        for fc in range(8):
                            t_w2 = p.dma("sync", w2g[:, fc, :],
                                         w2_bf[ei, :, (g * 8 + fc) * 128:(g * 8 + fc + 1) * 128],
                                         "sWG0" if wsl == 0 else "sWG1",
                                         transpose=True)
                        for nt in range(CT):
                            for fc in range(8):
                                pp = fc % 2
                                p.wait("tensor", [t_w1, t_xe, ph1_free[pp]])
                                for hc in range(8):
                                    th1 = p.op("tensor",
                                               lambda e, hc=hc, fc=fc, nt=nt, wsl=wsl, par=par, pp=pp:
                                               e.matmul(out=p_h1[pp][:],
                                                        lhsT=w1g[wsl][:, hc, fc * 128:(fc + 1) * 128],
                                                        rhs=xe[par][:, hc, nt * NT:(nt + 1) * NT],
                                                        start=(hc == 0), stop=(hc == 7)),
                                               inc="tT" if hc == 7 else None)
                                p.wait("scalar", [th1, h1_free])
                                tg1 = p.op("scalar", lambda e, fc=fc, g=g, ei=ei, pp=pp:
                                           e.activation(out=h1_sb[:, fc, :],
                                                        in_=p_h1[pp][:], func=AF.Gelu,
                                                        bias=bias_b1[:, ei, g * 8 + fc:g * 8 + fc + 1]),
                                           inc="tC")
                                ph1_free[pp] = tg1
                            t_h1 = p.tok("tC")
                            for hc in range(8):
                                pp2 = hc % 2
                                p.wait("tensor", [t_w2, t_h1, pm2e_free[pp2]])
                                for fc in range(8):
                                    tm2 = p.op("tensor",
                                               lambda e, hc=hc, fc=fc, pp2=pp2:
                                               e.matmul(out=p_m2e[pp2][:],
                                                        lhsT=w2g[:, fc, hc * 128:(hc + 1) * 128],
                                                        rhs=h1_sb[:, fc, :],
                                                        start=(fc == 0), stop=(fc == 7)),
                                               inc="tT" if fc == 7 else None)
                                if g == 0:
                                    p.wait("scalar", [tm2, m2acc_free])
                                    ev = p.op("scalar", lambda e, hc=hc, nt=nt, pp2=pp2:
                                              e.activation(out=m2acc[:, hc, nt * NT:(nt + 1) * NT],
                                                           in_=p_m2e[pp2][:],
                                                           func=AF.Identity), inc="tC")
                                else:
                                    p.wait("vector", [tm2, macc_tok[hc][nt]])
                                    ev = p.op("vector", lambda e, hc=hc, nt=nt, pp2=pp2:
                                              e.tensor_add(out=m2acc[:, hc, nt * NT:(nt + 1) * NT],
                                                           in0=m2acc[:, hc, nt * NT:(nt + 1) * NT],
                                                           in1=p_m2e[pp2][:]), inc="tV")
                                macc_tok[hc][nt] = ev
                                pm2e_free[pp2] = ev
                            h1_free = p.tok("tT")
                        w1_free[wsl] = p.tok("tT")
                        w2_free = p.tok("tT")
                    xe_free[par] = p.tok("tT")
                    t_m2all = [p.tok("tV"), p.tok("tC")]
                    for su in range(C // 128):
                        pt = su % 2
                        p.wait("tensor", [t_m2all, pye_free[pt]])
                        for hc in range(8):
                            tt = p.op("tensor", lambda e, hc=hc, su=su, pt=pt:
                                      e.transpose(out=p_ye[pt][:, hc * 128:(hc + 1) * 128],
                                                  in_=m2acc[:, hc, su * 128:(su + 1) * 128],
                                                  identity=ident_b[:]),
                                      inc="tT" if hc == 7 else None)
                        p.wait("vector", [tt, yest_free[pt]])
                        tev = p.op("vector", lambda e, pt=pt: e.tensor_copy(
                            out=ye_st[pt][:], in_=p_ye[pt][:]), inc="tV")
                        pye_free[pt] = tev
                        p.wait("sync", tev)
                        yest_free[pt] = p.dma("sync",
                                              y_sp[ei * C + su * 128:ei * C + (su + 1) * 128, :],
                                              ye_st[pt][:], "sY")
                    m2acc_free = p.tok("tT")
            p.barrier()

        if "B3" in phases:
            with contextlib.ExitStack() as sF:
                at_t = [sF.enter_context(sb(f"at_t{i}", [128, D], BF16)) for i in range(2)]
                ga = [sF.enter_context(sb(f"ga{i}", [128, D], BF16)) for i in range(2)]
                gb = [sF.enter_context(sb(f"gb{i}", [128, D], BF16)) for i in range(2)]
                x_c = [sF.enter_context(sb(f"x_c{i}", [128, D], F32)) for i in range(2)]
                tmpf = [sF.enter_context(sb(f"tmpf{i}", [128, D], F32)) for i in range(2)]

                t_y_all = p.tok("sY")
                t_as_all = p.tok("sAS")
                at_free = [None, None]
                g_free = [None, None]
                xc_free = [None, None]
                tmp_free = [None, None]
                for ti in range(NSUB):
                    par = ti % 2
                    p.wait("sync", [t_as_all, at_free[par]])
                    t_at = p.dma("sync", at_t[par][:],
                                 attn_sp[:, :, ti * 128:(ti + 1) * 128]
                                 .rearrange("c p n -> (c p) n"),
                                 "sAL", transpose=True)
                    p.wait("sync", xc_free[par])
                    t_x = p.dma("sync", x_c[par][:], x[ti * 128:(ti + 1) * 128, :],
                                "sX0" if par == 0 else "sX1")
                    p.wait("gpsimd", [t_y_all, g_free[par]])
                    p.op("gpsimd", lambda e, par=par, ti=ti: e.indirect_dma_start(
                        out=ga[par][:], out_offset=None,
                        in_=y_sp,
                        in_offset=IndirectOffsetOnAxis(ap=idx1u[:, ti:ti + 1], axis=0),
                        bounds_check=REG["bc"], oob_is_err=False), inc="sGA", inck=16)
                    tgb = p.op("gpsimd", lambda e, par=par, ti=ti: e.indirect_dma_start(
                        out=gb[par][:], out_offset=None,
                        in_=y_sp,
                        in_offset=IndirectOffsetOnAxis(ap=idx2u[:, ti:ti + 1], axis=0),
                        bounds_check=REG["bc"], oob_is_err=False), inc="sGA", inck=16)
                    p.wait("vector", [tgb, tmp_free[par]])
                    v1 = p.op("vector", lambda e, par=par, ti=ti: e.tensor_scalar(
                        out=tmpf[par][:], in0=ga[par][:], scalar1=wab[:, ti, 0:1],
                        scalar2=None, op0=ALU.mult), inc="tV")
                    p.wait("vector", v1)
                    v2 = p.op("vector", lambda e, par=par, ti=ti: e.scalar_tensor_tensor(
                        out=tmpf[par][:], in0=gb[par][:], scalar=wab[:, ti, 1:2],
                        in1=tmpf[par][:], op0=ALU.mult, op1=ALU.add), inc="tV")
                    g_free[par] = v2
                    p.wait("vector", [v2, t_at])
                    v3 = p.op("vector", lambda e, par=par: e.tensor_add(
                        out=tmpf[par][:], in0=tmpf[par][:], in1=at_t[par][:]), inc="tV")
                    at_free[par] = v3
                    p.wait("vector", [v3, t_x])
                    v4 = p.op("vector", lambda e, par=par: e.tensor_add(
                        out=x_c[par][:], in0=x_c[par][:], in1=tmpf[par][:]), inc="tV")
                    tmp_free[par] = v4
                    p.wait("sync", v4)
                    t_out = p.dma("sync", out[ti * 128:(ti + 1) * 128, :],
                                  x_c[par][:], "sOUT")
                    xc_free[par] = [v4, t_out]

        p.barrier()
        p.wait("sync", [(sm, p.n[sm]) for sm in p.n])
        p.check_deadlock()
        p.emit(blk)
    return nc, dbg


def kernel(**inputs):
    x = np.ascontiguousarray(np.asarray(inputs["L0_attn_x"]))
    weights = {
        "w_qkv": np.ascontiguousarray(np.asarray(inputs["L0_attn_w_qkv"])),
        "b_qkv": np.ascontiguousarray(np.asarray(inputs["L0_attn_b_qkv"])),
        "w_out": np.ascontiguousarray(np.asarray(inputs["L0_attn_w_out"])),
        "b_out": np.ascontiguousarray(np.asarray(inputs["L0_attn_b_out"])),
        "w_gate": np.ascontiguousarray(np.asarray(inputs["L0_attn_w_gate"])),
        "b_gate": np.ascontiguousarray(np.asarray(inputs["L0_attn_b_gate"])),
        "pos_bias": np.ascontiguousarray(np.asarray(inputs["L0_attn_pos_bias"])),
        "router_w": np.ascontiguousarray(np.asarray(inputs["L0_ffn_router_w"])),
        "w1": np.ascontiguousarray(np.asarray(inputs["L0_ffn_w1"])),
        "b1": np.ascontiguousarray(np.asarray(inputs["L0_ffn_b1"])),
        "w2": np.ascontiguousarray(np.asarray(inputs["L0_ffn_w2"])),
        "b2": np.ascontiguousarray(np.asarray(inputs["L0_ffn_b2"])),
    }
    nc, _dbg = build_nc(debug=False)
    in_maps = [dict(weights, x=np.ascontiguousarray(x[b])) for b in range(8)]
    res = run_bass_kernel_spmd(nc, in_maps, core_ids=list(range(8)))
    return np.stack([res.results[b]["out"] for b in range(8)], axis=0)


# version 09
# speedup vs baseline: 1.0728x; 1.0728x over previous
import contextlib
import numpy as np
import concourse.bass as bass
import concourse.mybir as mybir
from concourse.bass import IndirectOffsetOnAxis
from concourse.bass_utils import run_bass_kernel_spmd

F32 = mybir.dt.float32
BF16 = mybir.dt.bfloat16
U32 = mybir.dt.uint32
AF = mybir.ActivationFunctionType
ALU = mybir.AluOpType
AX = mybir.AxisListType

N = 4096
D = 1024
NH = 16
HD = 64
E = 8
FF = 4096
NT = 512
NTILES = N // NT
SUBS = NT // 128
HALF = N // 2

C = 1536
CT = C // NT
ECAP = 8 * C
BIG = float(1 << 20)
NSUB = N // 128

N_SCALES, N_TAPS = 11, 4
OFFSETS = [(1 << j) * tau for j in range(N_SCALES) for tau in range(N_TAPS)]
DIST = sorted(set(OFFSETS))
DIST_IDX = {o: i for i, o in enumerate(DIST)}
assert len(DIST) == 24

EPS = 1e-5
QSCALE = HD ** -0.5

DEBUG = False
ENGINES = ("sync", "gpsimd", "vector", "scalar", "tensor")


class P:

    def __init__(self, nc, es, semnames):
        self.nc = nc
        self.q = {e: [] for e in ENGINES}
        self.h = {s: es.enter_context(nc.semaphore(s)) for s in semnames}
        self.n = {s: 0 for s in semnames}

    def tok(self, sem):
        return (sem, self.n[sem])

    @staticmethod
    def _flat(toks, acc):
        if toks is None:
            return
        if isinstance(toks, tuple) and len(toks) == 2 and isinstance(toks[0], str):
            acc.append(toks)
            return
        for t in toks:
            P._flat(t, acc)

    def wait(self, eng, toks):
        acc = []
        P._flat(toks, acc)
        best = {}
        for sem, v in acc:
            best[sem] = max(best.get(sem, 0), v)
        for sem, v in best.items():
            assert v <= self.n[sem], f"forward wait on {sem}: {v} > {self.n[sem]}"
            if v > 0:
                self.q[eng].append(("w", sem, v))

    def fwait(self, eng, sem, val):
        self.q[eng].append(("w", sem, val))

    def op(self, eng, fn, inc=None, inck=1):
        self.q[eng].append(("o", fn, inc, inck))
        if inc is not None:
            self.n[inc] += inck
        return self.tok(inc) if inc else None

    def dma(self, eng, out, in_, sem, **kw):
        return self.op(eng, lambda e, o=out, i=in_, k=dict(kw): e.dma_start(out=o, in_=i, **k),
                       inc=sem, inck=16)

    def vchain(self, eng, fns, sem):
        t = None
        for fn in fns:
            if t is not None:
                self.wait(eng, t)
            t = self.op(eng, fn, inc=sem)
        return t

    def barrier(self, sems=None):
        if sems is None:
            sems = [sm for sm in self.n if sm not in ("qW", "sSC", "sXC")]
        toks = [(sm, self.n[sm]) for sm in sems]
        for eng in ENGINES:
            self.wait(eng, toks)

    def check_deadlock(self):
        pos = {e: 0 for e in ENGINES}
        cnt = {sm: 0 for sm in self.n}
        progress = True
        while progress:
            progress = False
            for eng in ENGINES:
                while pos[eng] < len(self.q[eng]):
                    it = self.q[eng][pos[eng]]
                    if it[0] == "w":
                        if cnt[it[1]] < it[2]:
                            break
                    else:
                        _, fn, inc, inck = it
                        if inc is not None:
                            cnt[inc] += inck
                    pos[eng] += 1
                    progress = True
        stuck = {e: (pos[e], len(self.q[e])) for e in ENGINES if pos[e] < len(self.q[e])}
        if stuck:
            for e, (i, n) in stuck.items():
                it = self.q[e][i]
                print(f"DEADLOCK {e} at op {i}/{n}: waiting {it[1]} >= {it[2]}, have {cnt[it[1]]}")
            raise AssertionError(f"deadlock: {stuck}")

    def emit(self, blk):
        for eng in ENGINES:
            ops = self.q[eng]
            if not ops:
                continue
            section = getattr(blk, eng)

            def body(e, ops=ops):
                for it in ops:
                    if it[0] == "w":
                        e.wait_ge(self.h[it[1]], it[2])
                    else:
                        _, fn, inc, inck = it
                        ins = fn(e)
                        if inc is not None:
                            ins.then_inc(self.h[inc], inck)
            section(body)


def build_nc(debug=False, phases=("A1", "IDX", "A2a", "B1", "B2", "B3"), races=False):
    nc = bass.Bass("TRN2", target_bir_lowering=False, detect_race_conditions=races)

    x = nc.dram_tensor("x", [N, D], F32, kind="ExternalInput").ap()
    w_qkv = nc.dram_tensor("w_qkv", [D, 3 * D], F32, kind="ExternalInput").ap()
    b_qkv = nc.dram_tensor("b_qkv", [3 * D], F32, kind="ExternalInput").ap()
    w_out = nc.dram_tensor("w_out", [D, D], F32, kind="ExternalInput").ap()
    b_out = nc.dram_tensor("b_out", [D], F32, kind="ExternalInput").ap()
    w_gate = nc.dram_tensor("w_gate", [D, D], F32, kind="ExternalInput").ap()
    b_gate = nc.dram_tensor("b_gate", [D], F32, kind="ExternalInput").ap()
    pos_bias = nc.dram_tensor("pos_bias", [44, NH], F32, kind="ExternalInput").ap()
    router_w = nc.dram_tensor("router_w", [E, D], F32, kind="ExternalInput").ap()
    w1 = nc.dram_tensor("w1", [E, FF, D], F32, kind="ExternalInput").ap()
    b1 = nc.dram_tensor("b1", [E, FF], F32, kind="ExternalInput").ap()
    w2 = nc.dram_tensor("w2", [E, D, FF], F32, kind="ExternalInput").ap()
    b2 = nc.dram_tensor("b2", [E, D], F32, kind="ExternalInput").ap()
    out = nc.dram_tensor("out", [N, D], F32, kind="ExternalOutput").ap()

    w1_bf = nc.dram_tensor("w1_bf", [E, FF, D], BF16, kind="Internal").ap()
    w2_bf = nc.dram_tensor("w2_bf", [E, D, FF], BF16, kind="Internal").ap()
    wq_perm = nc.dram_tensor("wq_perm", [8, 8, 128, 128], BF16, kind="Internal").ap()
    wk_perm = nc.dram_tensor("wk_perm", [8, 8, 128, 128], BF16, kind="Internal").ap()
    wv_perm = nc.dram_tensor("wv_perm", [8, 8, 128, 128], BF16, kind="Internal").ap()
    wg_perm = nc.dram_tensor("wg_perm", [8, 8, 128, 128], BF16, kind="Internal").ap()
    xn_sp = nc.dram_tensor("xn_sp", [N, D], BF16, kind="Internal").ap()
    gav_sp = nc.dram_tensor("gav_sp", [8, 128, N], BF16, kind="Internal").ap()
    pb_sp = nc.dram_tensor("pb_sp", [NH * 44], F32, kind="Internal").ap()
    expert_x = nc.dram_tensor("expert_x", [ECAP, D], BF16, kind="Internal").ap()
    y_sp = nc.dram_tensor("y_sp", [ECAP, D], BF16, kind="Internal").ap()
    attn_sp = nc.dram_tensor("attn_sp", [8, 128, N], BF16, kind="Internal").ap()
    cnt_dram = nc.dram_tensor("cnt_dram", [NSUB, E], F32, kind="Internal").ap()
    coef_sp = nc.dram_tensor("coef_sp", [E, N], BF16, kind="Internal").ap()
    base_dram = nc.dram_tensor("base_dram", [NSUB, E], F32, kind="Internal").ap()

    dbg = {}
    if debug:
        dbg["xn"] = nc.dram_tensor("dbg_xn", [N, D], F32, kind="ExternalOutput").ap()
        dbg["k"] = nc.dram_tensor("dbg_k", [8, 128, N], F32, kind="ExternalOutput").ap()
        dbg["v"] = nc.dram_tensor("dbg_v", [8, 128, N], F32, kind="ExternalOutput").ap()
        dbg["coef"] = nc.dram_tensor("dbg_coef", [E, N], F32, kind="ExternalOutput").ap()
        dbg["gav"] = nc.dram_tensor("dbg_gav", [8, 128, N], F32, kind="ExternalOutput").ap()
        dbg["sc"] = nc.dram_tensor("dbg_sc", [128, NH, 44], F32, kind="ExternalOutput").ap()
        dbg["q"] = nc.dram_tensor("dbg_q", [8, 128, NT], F32, kind="ExternalOutput").ap()
        dbg["av"] = nc.dram_tensor("dbg_av", [8, 128, NT], F32, kind="ExternalOutput").ap()
        dbg["qkt"] = nc.dram_tensor("dbg_qkt", [128, SUBS, 8, 80], F32, kind="ExternalOutput").ap()
        dbg["qks"] = nc.dram_tensor("dbg_qks", [80, NT], F32, kind="ExternalOutput").ap()

    es = contextlib.ExitStack()
    with es:
        sb = nc.sbuf_tensor
        ident_f = es.enter_context(sb("ident_f", [128, 128], F32))
        ident_b = es.enter_context(sb("ident_b", [128, 128], BF16))
        sel_b = es.enter_context(sb("sel_b", [128, NH], BF16))
        sel2_b = es.enter_context(sb("sel2_b", [80, 128], BF16))
        pb_bc = es.enter_context(sb("pb_bc", [128, NH, 44], F32))
        rwT = es.enter_context(sb("rwT", [128, 8, E], F32))
        bias_q = es.enter_context(sb("bias_q", [128, 8], F32))
        bias_k = es.enter_context(sb("bias_k", [128, 8], F32))
        bias_v = es.enter_context(sb("bias_v", [128, 8], F32))
        bias_g = es.enter_context(sb("bias_g", [128, 8], F32))
        bias_o = es.enter_context(sb("bias_o", [128, 8], F32))
        bias_b1 = es.enter_context(sb("bias_b1", [128, E, 32], F32))
        b2_bf = es.enter_context(sb("b2_bf", [E, D], BF16))
        ones_f = es.enter_context(sb("ones_f", [1, 128], F32))
        ones_b = es.enter_context(sb("ones_b", [1, 128], BF16))
        selt = es.enter_context(sb("selt", [128, NH], BF16))
        sel2t = es.enter_context(sb("sel2t", [NH, 128], BF16))
        wab = es.enter_context(sb("wab", [128, NSUB, 2], F32))
        idx1u = es.enter_context(sb("idx1u", [128, NSUB], U32))
        idx2u = es.enter_context(sb("idx2u", [128, NSUB], U32))
        xn_sc = [es.enter_context(sb("xn_sc0", [128, D], BF16))]
        kv_stack = contextlib.ExitStack()
        k_fm = kv_stack.enter_context(sb("k_fm", [128, 8, N], BF16))
        v_fm = kv_stack.enter_context(sb("v_fm", [128, 8, N], BF16))
        route_stack = contextlib.ExitStack()
        Lx = route_stack.enter_context(sb("Lx", [128, 128], F32))
        ones_col = route_stack.enter_context(sb("ones_col", [128, 1], F32))
        thr_row = route_stack.enter_context(sb("thr_row", [128, E], F32))
        mask1_all = route_stack.enter_context(sb("mask1_all", [128, NSUB, E], BF16))
        mask2_all = route_stack.enter_context(sb("mask2_all", [128, NSUB, E], BF16))
        pos_all = route_stack.enter_context(sb("pos_all", [128, NSUB, E], BF16))
        cnt_nat = route_stack.enter_context(sb("cnt_nat", [1, NSUB, E], F32))
        base_bc = route_stack.enter_context(sb("base_bc", [128, NSUB, E], F32))
        idx1f = route_stack.enter_context(sb("idx1f", [128, NSUB], F32))
        idx2f = route_stack.enter_context(sb("idx2f", [128, NSUB], F32))
        eoff32 = route_stack.enter_context(sb("eoff32", [32, E], F32))

        p = P(nc, es, ["qW", "tT", "tV", "tC", "tP", "sP0A", "sP0B", "sBIAS", "sWPERM",
                       "sWS0", "sWS1", "sX0", "sX1", "sWKV0", "sWKV1", "sXNSP",
                       "sGAV", "sCF", "sWO", "sWG0", "sWG1", "sDBG", "sDBG2", "sB2",
                       "sXNL0", "sXNL1", "sGV0", "sGV1", "sOUT",
                       "sIDX", "sXC", "sSC", "sEX0", "sEX1", "sY", "sGA", "sAS", "sAL",
                       "tGM"])
        blk = es.enter_context(nc.Block("mk"))

        def c_ident(t):
            def f(e, t=t):
                e.memset(t[:], 0.0)
                return e.affine_select(out=t[:], in_=t[:], compare_op=ALU.not_equal,
                                       fill=1.0, base=0, pattern=[[-1, 128]],
                                       channel_multiplier=1)
            return f
        p.op("gpsimd", c_ident(ident_f))
        p.op("gpsimd", c_ident(ident_b))
        p.op("gpsimd", lambda e: e.memset(ones_f[:], 1.0))
        p.op("gpsimd", lambda e: e.memset(ones_b[:], 1.0))

        def c_strictlow(e):
            e.memset(Lx[:], 0.0)
            return e.affine_select(out=Lx[:], in_=Lx[:], compare_op=ALU.is_ge,
                                   fill=1.0, base=0, pattern=[[-1, 128]],
                                   channel_multiplier=1)
        p.op("gpsimd", c_strictlow)

        REG = {}

        def c_bcreg(e):
            g = e.register("bcreg")
            REG["guard"] = g
            REG["bc"] = g.__enter__()
            return e.reg_mov(REG["bc"], ECAP - 1)
        p.op("gpsimd", c_bcreg)
        p.op("gpsimd", lambda e: e.memset(ones_col[:], 1.0))
        p.op("gpsimd", lambda e: e.iota(out=thr_row[:], pattern=[[C, E]], base=C,
                                        channel_multiplier=0,
                                        allow_small_or_imprecise_dtypes=True))
        p.op("gpsimd", lambda e: e.iota(out=eoff32[:], pattern=[[C, E]], base=0,
                                        channel_multiplier=0,
                                        allow_small_or_imprecise_dtypes=True))
        p.op("gpsimd", lambda e: e.iota(out=sel_b[:], pattern=[[-8, NH]], base=0,
                                        channel_multiplier=1,
                                        allow_small_or_imprecise_dtypes=True))
        t_consts = p.op("gpsimd", lambda e: e.iota(out=sel2_b[0:NH, :], pattern=[[1, 128]],
                                                   base=0, channel_multiplier=-8,
                                                   allow_small_or_imprecise_dtypes=True),
                        inc="tP")

        p.wait("vector", t_consts)
        t_sel = p.vchain("vector", [
            lambda e: e.tensor_scalar(
                out=selt[:], in0=sel_b[:], scalar1=1.0, scalar2=0.0,
                op0=ALU.mult, op1=ALU.is_ge),
            lambda e: e.tensor_scalar(
                out=sel_b[:], in0=sel_b[:], scalar1=-1.0, scalar2=-7.0,
                op0=ALU.mult, op1=ALU.is_ge),
            lambda e: e.tensor_mul(out=sel_b[:], in0=sel_b[:], in1=selt[:]),
            lambda e: e.tensor_scalar(
                out=sel2t[:], in0=sel2_b[0:NH, :], scalar1=1.0, scalar2=0.0,
                op0=ALU.mult, op1=ALU.is_ge),
            lambda e: e.tensor_scalar(
                out=sel2_b[0:NH, :], in0=sel2_b[0:NH, :], scalar1=-1.0, scalar2=-7.0,
                op0=ALU.mult, op1=ALU.is_ge),
            lambda e: e.tensor_mul(
                out=sel2_b[0:NH, :], in0=sel2_b[0:NH, :], in1=sel2t[:]),
        ], "tV")
        p.wait("sync", t_sel)
        p.dma("sync", sel2_b[32:32 + NH, :], sel2_b[0:NH, :], "sP0A")
        t_sel2rep = p.dma("sync", sel2_b[64:64 + NH, :], sel2_b[0:NH, :], "sP0A")

        for h in range(NH):
            p.dma("sync", pb_sp[h * 44:(h + 1) * 44][None, :],
                  pos_bias[:, h][None, :], "sP0A", allow_slow_non_contiguous=True)
        t_pb_sp = p.tok("sP0A")

        bqkv_r = b_qkv.rearrange("(s hh r c) -> s (hh r) c", s=3, hh=16, r=8, c=8)
        bg_r = b_gate.rearrange("(hh r c) -> (hh r) c", hh=16, r=8, c=8)
        p.dma("sync", bias_q[:], bqkv_r[0], "sBIAS")
        p.dma("sync", bias_k[:], bqkv_r[1], "sBIAS")
        p.dma("sync", bias_v[:], bqkv_r[2], "sBIAS")
        p.dma("sync", bias_g[:], bg_r, "sBIAS")
        p.dma("gpsimd", b2_bf[:], b2, "sB2")
        t_bias = p.tok("sBIAS")
        p.wait("vector", t_bias)
        t_bias = [t_bias, p.op("vector", lambda e: e.tensor_scalar(
            out=bias_q[:], in0=bias_q[:], scalar1=QSCALE, scalar2=None,
            op0=ALU.mult), inc="tV")]

        with contextlib.ExitStack() as p0s:
            pb_flat = p0s.enter_context(sb("pb_flat", [1, NH * 44], F32))
            rw_nat = p0s.enter_context(sb("rw_nat", [E, D], F32))
            b1_nat = p0s.enter_context(sb("b1_nat", [E, FF], F32))
            bo_nat = p0s.enter_context(sb("bo_nat", [8, 128], F32))
            wstage = [p0s.enter_context(sb(f"wstage{i}", [128, D], BF16)) for i in range(2)]
            wperm = [p0s.enter_context(sb(f"wperm{i}", [128, 8, 128], BF16)) for i in range(2)]
            p_tr8 = p0s.enter_context(nc.psum_tensor("p_tr8", [128, E], F32))
            p_pb = p0s.enter_context(nc.psum_tensor("p_pb", [128, 512], F32))
            p_pb2 = p0s.enter_context(nc.psum_tensor("p_pb2", [128, 192], F32))

            p.wait("sync", t_pb_sp)
            p.dma("sync", pb_flat[:], pb_sp[None, :], "sP0B")
            p.dma("sync", rw_nat[:], router_w, "sP0B")
            p.dma("sync", b1_nat[:], b1, "sP0B")
            t_p0l = p.dma("sync", bo_nat[:], b_out.rearrange("(a b) -> a b", a=8), "sP0B")

            p.wait("tensor", t_p0l)
            p.op("tensor", lambda e: e.matmul(out=p_pb[:], lhsT=ones_f[:],
                                              rhs=pb_flat[:, 0:512], start=True, stop=True))
            t_pbmm = p.op("tensor", lambda e: e.matmul(out=p_pb2[:], lhsT=ones_f[:],
                                                       rhs=pb_flat[:, 512:704],
                                                       start=True, stop=True), inc="tT")
            p.wait("scalar", t_pbmm)
            pbf = pb_bc[:].rearrange("p h i -> p (h i)")
            p.op("scalar", lambda e: e.copy(out=pbf[:, 0:512], in_=p_pb[:]))
            t_pbdone = p.op("scalar", lambda e: e.copy(out=pbf[:, 512:704], in_=p_pb2[:]),
                            inc="tC")

            def small_tr(src_ap, dst_ap):
                kk = src_ap.shape[0]
                tt = p.op("tensor", lambda e, s=src_ap, kk=kk: e.transpose(
                    out=p_tr8[:, 0:E], in_=s, identity=ident_f[0:kk, 0:kk]), inc="tT")
                p.wait("scalar", tt)
                tc = p.op("scalar", lambda e, d=dst_ap: e.copy(out=d, in_=p_tr8[:, 0:E]),
                          inc="tC")
                p.wait("tensor", tc)

            for hc in range(8):
                small_tr(rw_nat[:, hc * 128:(hc + 1) * 128], rwT[:, hc, :])
            for fc in range(32):
                small_tr(b1_nat[:, fc * 128:(fc + 1) * 128], bias_b1[:, :, fc])
            small_tr(bo_nat[:, 0:128], bias_o[:, :])

            stage_free = [None, None]
            perm_free = [None, None]
            cnt = 0
            for sec in range(4):
                wdst = (wq_perm, wk_perm, wv_perm, wg_perm)[sec]
                for hc in range(8):
                    si = cnt % 2
                    src = (w_qkv[hc * 128:(hc + 1) * 128, sec * D:(sec + 1) * D]
                           if sec < 3 else w_gate[hc * 128:(hc + 1) * 128, :])
                    p.wait("gpsimd", stage_free[si])
                    t_stage = p.dma("gpsimd", wstage[si][:], src, "sWS0" if si == 0 else "sWS1")
                    p.wait("vector", [t_stage, perm_free[si]])
                    t_perm = p.op("vector", lambda e, si=si: e.tensor_copy(
                        out=wperm[si][:].rearrange("p c (hh r) -> p c hh r", hh=16, r=8),
                        in_=wstage[si][:].rearrange("p (hh r c) -> p c hh r", hh=16, r=8, c=8)),
                        inc="tV")
                    stage_free[si] = t_perm
                    p.wait("sync", t_perm)
                    perm_free[si] = p.dma("sync",
                                          wdst[:, hc, :, :].rearrange("c p m -> p c m"),
                                          wperm[si][:], "sWPERM")
                    cnt += 1
            t_wperm = p.tok("sWPERM")


        p.barrier()
        t_kv_done = None
        if "A1" in phases:
            with contextlib.ExitStack() as sA:
                coef_bf = sA.enter_context(sb("coef_bf", [E, N], BF16))
                x_tm = [sA.enter_context(sb(f"x_tm{i}", [128, D], F32)) for i in range(2)]
                sq_t = sA.enter_context(sb("sq_t", [128, D], F32))
                stat = [sA.enter_context(sb(f"stat{i}", [128, 8], F32)) for i in range(2)]
                xn_tm_b = [sA.enter_context(sb(f"xn_tm_b{i}", [128, D], BF16)) for i in range(2)]
                xn_tm_f = [sA.enter_context(sb(f"xn_tm_f{i}", [128, D], F32)) for i in range(2)]
                xnf_fm = [sA.enter_context(sb(f"xnf_fm{i}", [128, 8, 128], F32)) for i in range(2)]
                xn_fm = [sA.enter_context(sb(f"xn_fm{i}", [128, 8, NT], BF16)) for i in range(2)]
                logit_f = [sA.enter_context(sb(f"logit_f{i}", [E, 128], F32)) for i in range(2)]
                logit_tm = [sA.enter_context(sb(f"logit_tm{i}", [128, E], F32)) for i in range(2)]
                topk = [sA.enter_context(sb(f"topk{i}", [128, 8], F32)) for i in range(2)]
                eqm = [sA.enter_context(sb(f"eqm{i}", [128, E], F32)) for i in range(2)]
                eqm2 = [sA.enter_context(sb(f"eqm2{i}", [128, E], F32)) for i in range(2)]
                msum = [sA.enter_context(sb(f"msum{i}", [128, E], F32)) for i in range(2)]
                coef_tm = [sA.enter_context(sb(f"coef_tm{i}", [128, E], F32)) for i in range(2)]
                cstage = sA.enter_context(sb("cstage", [E, 128], F32)) if debug else None
                wkv_buf = [sA.enter_context(sb(f"wkv{i}", [128, 8, 128], BF16)) for i in range(2)]
                p_xnT = [sA.enter_context(nc.psum_tensor(f"p_xnT{i}", [128, 512], F32))
                         for i in range(2)]
                p_rt = sA.enter_context(nc.psum_tensor("p_rt", [E, 128], F32))
                p_rtT = sA.enter_context(nc.psum_tensor("p_rtT", [128, E], F32))
                p_cT = sA.enter_context(nc.psum_tensor("p_cT", [E, 128], F32))
                p_kv = [sA.enter_context(nc.psum_tensor(f"p_kv{i}", [128, NT], F32))
                        for i in range(2)]
                p_pos = sA.enter_context(nc.psum_tensor("p_pos", [128, 2 * E], F32))

                x_free = [None, None]
                xnb_free = [None, None]
                xnfm_free = [None, None]
                xnfm_ready = [None, None]
                wkv_free = [None, None]
                pxnT_free = [None, None]
                pkv_free = [None, None]
                xn_tm_f_free = [None, None]
                xnf_fm_free = [None, None]
                p_rt_free = None
                p_rtT_free = None
                p_cT_free = None
                msum_free = [None, None]
                lgt_free = [None, None]
                sq_free = [None]
                p_pos_free = None

                SS = {}

                def a1_h1(ss):
                    t, s = ss // SUBS, ss % SUBS
                    fmslot = t % 2
                    if True:
                        xslot = ss % 2
                        row0 = t * NT + s * 128
                        p.wait("sync", x_free[xslot])
                        tx = p.dma("sync", x_tm[xslot][:], x[row0:row0 + 128, :],
                                   "sX0" if xslot == 0 else "sX1")
                        xs = xslot
                        st_, sq_, xnf_, xff_ = stat[xs], sq_t, xn_tm_f[xs], xnf_fm[xs]
                        lgf_, lgt_, tk_ = logit_f[xs], logit_tm[xs], topk[xs]
                        eq_, eq2_, ms_, cf_ = eqm[xs], eqm2[xs], msum[xs], coef_tm[xs]
                        p.wait("scalar", [tx, sq_free[0]])
                        tsq = p.op("scalar", lambda e, xs=xs, sq_=sq_: e.activation(
                            out=sq_[:], in_=x_tm[xs][:], func=AF.Square), inc="tC")
                        p.wait("vector", tx)
                        p.op("vector", lambda e, xs=xs, st_=st_: e.reduce_sum(
                            out=st_[:, 0:1], in_=x_tm[xs][:], axis=AX.X))
                        p.wait("vector", tsq)
                        tred = p.op("vector", lambda e, st_=st_, sq_=sq_: e.reduce_sum(
                            out=st_[:, 1:2], in_=sq_[:], axis=AX.X), inc="tV")
                        sq_free[0] = tred
                        p.wait("vector", tred)
                        tvar = p.vchain("vector", [
                            lambda e, st_=st_: e.tensor_scalar(
                                out=st_[:, 2:3], in0=st_[:, 0:1], scalar1=1.0 / D,
                                scalar2=None, op0=ALU.mult),
                            lambda e, st_=st_: e.tensor_mul(
                                out=st_[:, 6:7], in0=st_[:, 2:3], in1=st_[:, 2:3]),
                            lambda e, st_=st_: e.scalar_tensor_tensor(
                                out=st_[:, 3:4], in0=st_[:, 1:2], scalar=1.0 / D,
                                in1=st_[:, 6:7], op0=ALU.mult, op1=ALU.subtract),
                            lambda e, st_=st_: e.tensor_scalar(
                                out=st_[:, 3:4], in0=st_[:, 3:4], scalar1=EPS,
                                scalar2=None, op0=ALU.add),
                        ], "tV")
                        p.wait("scalar", tvar)
                        trstd = p.vchain("scalar", [
                            lambda e, st_=st_: e.activation(
                                out=st_[:, 7:8], in_=st_[:, 3:4], func=AF.Ln),
                            lambda e, st_=st_: e.activation(
                                out=st_[:, 4:5], in_=st_[:, 7:8], func=AF.Exp,
                                scale=-0.5),
                        ], "tC")
                        p.wait("vector", [trstd, xnb_free[xslot], xn_tm_f_free[xs]])
                        rstd_bc = st_[:, 4:5].broadcast_to([128, D])
                        p.op("vector", lambda e, xs=xs, rb=rstd_bc, st_=st_: e.scalar_tensor_tensor(
                            out=xn_tm_b[xs][:], in0=x_tm[xs][:], scalar=st_[:, 2:3],
                            in1=rb, op0=ALU.subtract, op1=ALU.mult))
                        txn = p.op("vector", lambda e, xs=xs, rb=rstd_bc, st_=st_, xnf_=xnf_:
                                   e.scalar_tensor_tensor(
                            out=xnf_[:], in0=x_tm[xs][:], scalar=st_[:, 2:3],
                            in1=rb, op0=ALU.subtract, op1=ALU.mult), inc="tV")
                        x_free[xslot] = [txn, tred]
                        p.wait("sync", txn)
                        xnb_free[xslot] = p.dma("sync", xn_sp[row0:row0 + 128, :],
                                                xn_tm_b[xslot][:], "sXNSP")
                        if debug:
                            p.dma("sync", dbg["xn"][row0:row0 + 128, :], xnf_[:], "sDBG")
                        SS[ss] = dict(txn=txn)

                def a1_h2(ss):
                    nonlocal p_rt_free, p_rtT_free
                    t, s = ss // SUBS, ss % SUBS
                    fmslot = t % 2
                    if True:
                        xslot = ss % 2
                        row0 = t * NT + s * 128
                        xs = xslot
                        xnf_, xff_, lgf_, lgt_ = xn_tm_f[xs], xnf_fm[xs], logit_f[xs], logit_tm[xs]
                        txn = SS[ss]["txn"]
                        ttr_last = None
                        for g in range(2):
                            p.wait("tensor", [txn, pxnT_free[g]])
                            for hc4 in range(4):
                                hc = g * 4 + hc4
                                last = (hc4 == 3)
                                ttr = p.op("tensor", lambda e, hc=hc, hc4=hc4, g=g, xnf_=xnf_:
                                           e.transpose(
                                    out=p_xnT[g][:, hc4 * 128:(hc4 + 1) * 128],
                                    in_=xnf_[:, hc * 128:(hc + 1) * 128],
                                    identity=ident_f[:]), inc="tT" if last else None)
                            ttr_last = ttr
                            p.wait("scalar", ttr)
                            tec = p.op("scalar", lambda e, g=g, s=s, fs=fmslot: e.copy(
                                out=xn_fm[fs][:, 4 * g:4 * (g + 1), s * 128:(s + 1) * 128],
                                in_=p_xnT[g][:].rearrange("p (c n) -> p c n", c=4)), inc="tC")
                            p.wait("vector", [tec, xnf_fm_free[xs]])
                            tev = p.op("vector", lambda e, g=g, xff_=xff_: e.tensor_copy(
                                out=xff_[:, 4 * g:4 * (g + 1), :],
                                in_=p_xnT[g][:].rearrange("p (c n) -> p c n", c=4)), inc="tV")
                            pxnT_free[g] = [tec, tev]
                        xn_tm_f_free[xs] = [ttr_last] + ([p.tok("sDBG")] if debug else [])
                        t_xnfm_sub = p.tok("tC")
                        t_xnf = p.tok("tV")
                        p.wait("tensor", [t_xnf, t_sel, p_rt_free])
                        for hc in range(8):
                            trt = p.op("tensor", lambda e, hc=hc, xff_=xff_: e.matmul(
                                out=p_rt[:], lhsT=rwT[:, hc, :], rhs=xff_[:, hc, :],
                                start=(hc == 0), stop=(hc == 7)),
                                inc="tT" if hc == 7 else None)
                        xnf_fm_free[xs] = trt
                        p.wait("scalar", trt)
                        p_rt_free = p.op("scalar", lambda e, lgf_=lgf_: e.copy(
                            out=lgf_[:], in_=p_rt[:]), inc="tC")
                        p.wait("tensor", [p_rt_free, p_rtT_free])
                        trtT = p.op("tensor", lambda e, lgf_=lgf_: e.transpose(
                            out=p_rtT[:], in_=lgf_[:], identity=ident_f[0:E, 0:E]), inc="tT")
                        p.wait("vector", [trtT, lgt_free[xs]])
                        p_rtT_free = p.op("vector", lambda e, lgt_=lgt_: e.tensor_copy(
                            out=lgt_[:], in_=p_rtT[:]), inc="tV")
                        tl = p_rtT_free
                        lgt_free[xs] = None
                        SS[ss]["tl"] = tl

                def a1_h3(ss):
                    nonlocal p_cT_free, p_pos_free
                    t, s = ss // SUBS, ss % SUBS
                    if True:
                        xslot = ss % 2
                        row0 = t * NT + s * 128
                        xs = xslot
                        lgt_, tk_ = logit_tm[xs], topk[xs]
                        eq_, eq2_, ms_, cf_ = eqm[xs], eqm2[xs], msum[xs], coef_tm[xs]
                        tl = SS.pop(ss)["tl"]
                        p.wait("vector", tl)
                        tdiff = p.vchain("vector", [
                            lambda e, tk_=tk_, lgt_=lgt_: e.reduce_max(
                                out=tk_[:, 0:1], in_=lgt_[:], axis=AX.X),
                            lambda e, tk_=tk_, lgt_=lgt_, eq_=eq_: e.tensor_scalar(
                                out=eq_[:], in0=lgt_[:], scalar1=tk_[:, 0:1],
                                scalar2=None, op0=ALU.is_ge),
                            lambda e, eq_=eq_, eq2_=eq2_, lgt_=lgt_: e.scalar_tensor_tensor(
                                out=eq2_[:], in0=eq_[:], scalar=-1e30, in1=lgt_[:],
                                op0=ALU.mult, op1=ALU.add),
                            lambda e, tk_=tk_, eq2_=eq2_: e.reduce_max(
                                out=tk_[:, 1:2], in_=eq2_[:], axis=AX.X),
                            lambda e, tk_=tk_: e.tensor_sub(
                                out=tk_[:, 6:7], in0=tk_[:, 1:2], in1=tk_[:, 0:1]),
                        ], "tV")
                        lgt_free[xs] = tdiff
                        p.wait("scalar", tdiff)
                        tw2r = p.op("scalar", lambda e, tk_=tk_: e.activation(
                            out=tk_[:, 2:3], in_=tk_[:, 6:7], func=AF.Exp), inc="tC")
                        ti = t * SUBS + s
                        p.wait("vector", [tw2r, msum_free[xs]])
                        tcoef = p.vchain("vector", [
                            lambda e, tk_=tk_: e.tensor_scalar(
                                out=tk_[:, 3:4], in0=tk_[:, 2:3], scalar1=1.0,
                                scalar2=None, op0=ALU.add),
                            lambda e, tk_=tk_: e.reciprocal(
                                out=tk_[:, 4:5], in_=tk_[:, 3:4]),
                            lambda e, tk_=tk_: e.tensor_mul(
                                out=tk_[:, 5:6], in0=tk_[:, 2:3], in1=tk_[:, 4:5]),
                            lambda e, tk_=tk_, eq2_=eq2_: e.tensor_scalar(
                                out=eq2_[:], in0=eq2_[:], scalar1=tk_[:, 1:2],
                                scalar2=None, op0=ALU.is_ge),
                            lambda e, ti=ti, eq2_=eq2_: e.tensor_copy(
                                out=mask2_all[:, ti, :], in_=eq2_[:]),
                            lambda e, ms_=ms_, eq_=eq_, eq2_=eq2_: e.tensor_add(
                                out=ms_[:], in0=eq_[:], in1=eq2_[:]),
                            lambda e, tk_=tk_, eq2_=eq2_: e.tensor_scalar(
                                out=eq2_[:], in0=eq2_[:], scalar1=tk_[:, 5:6],
                                scalar2=None, op0=ALU.mult),
                            lambda e, cf_=cf_, eq_=eq_, tk_=tk_, eq2_=eq2_: e.scalar_tensor_tensor(
                                out=cf_[:], in0=eq_[:], scalar=tk_[:, 4:5],
                                in1=eq2_[:], op0=ALU.mult, op1=ALU.add),
                            lambda e, ti=ti, eq_=eq_: e.tensor_copy(
                                out=mask1_all[:, ti, :], in_=eq_[:]),
                            lambda e, ti=ti, tk_=tk_: e.tensor_copy(
                                out=wab[:, ti, :], in_=tk_[:, 4:6]),
                        ], "tV")
                        p.wait("tensor", [tcoef, t_consts, p_pos_free])
                        p.op("tensor", lambda e, ms_=ms_: e.matmul(
                            out=p_pos[:, 0:E], lhsT=Lx[:], rhs=ms_[:],
                            start=True, stop=True))
                        tposm = p.op("tensor", lambda e, ms_=ms_: e.matmul(
                            out=p_pos[0:1, E:2 * E], lhsT=ones_col[:], rhs=ms_[:],
                            start=True, stop=True), inc="tT")
                        msum_free[xs] = tposm
                        p.wait("scalar", tposm)
                        p.op("scalar", lambda e, ti=ti: e.copy(
                            out=pos_all[:, ti, :], in_=p_pos[:, 0:E]))
                        tpose = p.op("scalar", lambda e, ti=ti: e.copy(
                            out=cnt_nat[:, ti, :], in_=p_pos[0:1, E:2 * E]), inc="tC")
                        p_pos_free = tpose
                        p.wait("tensor", [tcoef, p_cT_free])
                        tcT = p.op("tensor", lambda e, cf_=cf_: e.transpose(
                            out=p_cT[:], in_=cf_[:], identity=ident_f[:]), inc="tT")
                        p.wait("scalar", tcT)
                        tcbf = p.op("scalar", lambda e, r0=row0: e.copy(
                            out=coef_bf[:, r0:r0 + 128], in_=p_cT[:]), inc="tC")
                        p_cT_free = [tcbf]
                        if debug:
                            tdc = p.op("scalar", lambda e: e.copy(
                                out=cstage[:], in_=p_cT[:]), inc="tC")
                            p_cT_free.append(tdc)
                            p.wait("sync", tdc)
                            tdcd = p.dma("sync", dbg["coef"][:, row0:row0 + 128],
                                         cstage[:], "sDBG")
                def emit_kv(t):
                    fmslot = t % 2
                    xnfm_ready[fmslot] = p.tok("tC")
                    p.wait("sync", t_wperm)
                    tkv_mm = None
                    for kv in range(2):
                        wsrc = (wk_perm, wv_perm)[kv]
                        dst = (k_fm, v_fm)[kv]
                        bias = (bias_k, bias_v)[kv]
                        for mc in range(8):
                            i = kv * 8 + mc
                            wslot = i % 2
                            pslot = i % 2
                            p.wait("sync", wkv_free[wslot])
                            twl = p.dma("sync", wkv_buf[wslot][:],
                                        wsrc[mc].rearrange("h p m -> p h m"),
                                        "sWKV0" if wslot == 0 else "sWKV1")
                            p.wait("tensor", [twl, xnfm_ready[fmslot], pkv_free[pslot]])
                            for hc in range(8):
                                tkv_mm = p.op(
                                    "tensor",
                                    lambda e, hc=hc, ws=wslot, ps=pslot, fs=fmslot:
                                    e.matmul(out=p_kv[ps][:],
                                             lhsT=wkv_buf[ws][:, hc, :],
                                             rhs=xn_fm[fs][:, hc, :],
                                             start=(hc == 0), stop=(hc == 7)),
                                    inc="tT" if hc == 7 else None)
                            wkv_free[wslot] = tkv_mm
                            p.wait("vector", tkv_mm)
                            pkv_free[pslot] = p.op(
                                "vector",
                                lambda e, d=dst, mc=mc, t=t, b=bias, ps=pslot:
                                e.tensor_scalar(out=d[:, mc, t * NT:(t + 1) * NT],
                                                in0=p_kv[ps][:], scalar1=b[:, mc:mc + 1],
                                                scalar2=None, op0=ALU.add), inc="tV")
                    xnfm_free[fmslot] = tkv_mm

                for ss in range(NSUB + 2):
                    if ss < NSUB:
                        a1_h1(ss)
                    if 1 <= ss <= NSUB:
                        a1_h2(ss - 1)
                        if (ss - 1) % SUBS == SUBS - 1:
                            emit_kv((ss - 1) // SUBS)
                    if ss >= 2:
                        a1_h3(ss - 2)

                p.wait("sync", p.tok("tC"))
                p.dma("sync", coef_sp, coef_bf[:], "sCF")
                t_kv_done = [p.tok("tC"), p.tok("tV")]
                if debug:
                    p.wait("gpsimd", t_kv_done)
                    for c in range(8):
                        p.dma("gpsimd", dbg["k"][c], k_fm[:, c, :], "sDBG2")
                        p.dma("gpsimd", dbg["v"][c], v_fm[:, c, :], "sDBG2")

        p.barrier()
        if "IDX" in phases:
            with contextlib.ExitStack() as sI:
                cnt32 = sI.enter_context(sb("cnt32", [32, E], F32))
                base32 = sI.enter_context(sb("base32", [32, E], F32))
                base_row = sI.enter_context(sb("base_row", [1, NSUB * E], F32))
                gtmp = sI.enter_context(sb("gtmp", [128, E], F32))
                ovf = sI.enter_context(sb("ovf", [128, E], F32))
                mr = sI.enter_context(sb("mr", [128, E], F32))
                p_idx = sI.enter_context(nc.psum_tensor("p_idx", [32, E], F32))
                p_bb = sI.enter_context(nc.psum_tensor("p_bb", [128, NSUB * E], F32))

                t_eoff = t_consts
                p.dma("sync", cnt_dram.rearrange("a b -> (a b)")[None, :],
                      cnt_nat[:].rearrange("p a b -> p (a b)"), "sIDX")
                p.wait("sync", p.tok("sIDX"))
                t_c32 = p.dma("sync", cnt32[:], cnt_dram, "sIDX")
                p.wait("tensor", t_c32)
                t_pfx = p.op("tensor", lambda e: e.matmul(
                    out=p_idx[:], lhsT=Lx[0:32, 0:32], rhs=cnt32[:],
                    start=True, stop=True), inc="tT")
                p.wait("scalar", t_pfx)
                t_b32 = p.op("scalar", lambda e: e.copy(out=base32[:], in_=p_idx[:]),
                             inc="tC")
                p.wait("vector", [t_b32, t_eoff])
                t_badd = p.op("vector", lambda e: e.tensor_add(
                    out=base32[:], in0=base32[:], in1=eoff32[:]), inc="tV")
                p.wait("sync", t_badd)
                p.dma("sync", base_dram, base32[:], "sIDX")
                p.wait("sync", p.tok("sIDX"))
                t_brow = p.dma("sync", base_row[:],
                               base_dram.rearrange("a b -> (a b)")[None, :], "sIDX")
                p.wait("tensor", t_brow)
                t_bbm = p.op("tensor", lambda e: e.matmul(
                    out=p_bb[:], lhsT=ones_f[:], rhs=base_row[:],
                    start=True, stop=True), inc="tT")
                p.wait("scalar", t_bbm)
                t_bbc = p.op("scalar", lambda e: e.copy(
                    out=base_bc[:].rearrange("p a b -> p (a b)"), in_=p_bb[:]),
                    inc="tC")
                p.wait("vector", t_bbc)
                for ti in range(NSUB):
                    p.vchain("vector", [
                        lambda e, ti=ti: e.tensor_add(
                            out=gtmp[:], in0=base_bc[:, ti, :], in1=pos_all[:, ti, :]),
                        lambda e: e.tensor_tensor(
                            out=ovf[:], in0=gtmp[:], in1=thr_row[:], op=ALU.is_ge),
                        lambda e: e.scalar_tensor_tensor(
                            out=gtmp[:], in0=ovf[:], scalar=BIG, in1=gtmp[:],
                            op0=ALU.mult, op1=ALU.add),
                        lambda e, ti=ti: e.tensor_mul(
                            out=mr[:], in0=mask1_all[:, ti, :], in1=gtmp[:]),
                        lambda e, ti=ti: e.reduce_sum(
                            out=idx1f[:, ti:ti + 1], in_=mr[:], axis=AX.X),
                        lambda e, ti=ti: e.tensor_mul(
                            out=mr[:], in0=mask2_all[:, ti, :], in1=gtmp[:]),
                        lambda e, ti=ti: e.reduce_sum(
                            out=idx2f[:, ti:ti + 1], in_=mr[:], axis=AX.X),
                    ], "tV")
                t_idxc = p.vchain("vector", [
                    lambda e: e.tensor_copy(out=idx1u[:], in_=idx1f[:]),
                    lambda e: e.tensor_copy(out=idx2u[:], in_=idx2f[:]),
                ], "tV")

            t_xnsp_all = p.tok("sXNSP")
            t_idx_ready = t_idxc
            xnsc_free = [None]
            p.barrier()
        route_stack.close()

        if "A2a" in phases:
            with contextlib.ExitStack() as sB:
                xn2 = sB.enter_context(sb("xn2", [128, 8, NT], BF16))
                wqg_buf = [sB.enter_context(sb(f"wqg{i}", [128, 8, 128], BF16)) for i in range(2)]
                q_fm = sB.enter_context(sb("q_fm", [128, 8, NT], BF16))
                gate_fm = [sB.enter_context(sb(f"gate_fm{i}", [128, 8, NT], BF16)) for i in range(2)]
                prod_sb = [sB.enter_context(sb(f"prod{i}", [128, NT], BF16)) for i in range(2)]
                qk_stage = [sB.enter_context(sb(f"qkst{i}", [80, NT], BF16)) for i in range(2)]
                qkT_sb = sB.enter_context(sb("qkT", [128, SUBS, 8, 80], BF16))
                scores_tm = sB.enter_context(sb("scores_tm", [128, NH, 44], F32))
                sm_stat = sB.enter_context(sb("sm_stat", [128, NH, 2], F32))
                attn_coll = sB.enter_context(sb("attn_coll", [128, 24, NH], BF16))
                attnT = sB.enter_context(sb("attnT", [80, 8, NT], BF16))
                bcast_sb = [sB.enter_context(sb(f"bcs{i}", [128, NT], BF16)) for i in range(2)]
                avtmp = sB.enter_context(sb("avtmp", [128, NT], BF16))
                avacc = sB.enter_context(sb("avacc", [128, 8, NT], BF16))
                p_bc = [sB.enter_context(nc.psum_tensor(f"p_bc{i}", [128, NT], F32)) for i in range(2)]
                p_sc = [sB.enter_context(nc.psum_tensor(f"p_sc{i}", [NH, NT], F32)) for i in range(2)]
                p_av = [sB.enter_context(nc.psum_tensor(f"p_av{i}", [128, NT], F32)) for i in range(2)]
                p_qkT1 = sB.enter_context(nc.psum_tensor("p_qkT1", [128, 80], BF16))
                p_at1 = sB.enter_context(nc.psum_tensor("p_at1", [NH, 128], BF16))

                def p_qkT_v(pq):
                    return p_qkT1[:]

                def p_at_v(pq):
                    return p_at1[:]

                t_spills = p.tok("sXNSP")
                SS2 = {}
                wqg_free = [None, None]
                pbc_free = [None, None]
                psc_free = [None, None]
                pqkT_free = [None, None]
                pat_free = [None, None]
                bcs_free = [None, None]
                prod_free = [None, None]
                qkst_free = [None, None]
                pav_free = [None, None]
                xn2_free = None
                qg_evac_last = None
                avacc_free = None
                attnT_free = None

                def a2a_q(t):
                    nonlocal xn2_free, qg_evac_last
                    c0 = t * NT
                    p.wait("sync", [t_spills, xn2_free])
                    for hc in range(8):
                        txn2 = p.dma("sync", xn2[:, hc, :],
                                     xn_sp[c0:c0 + NT, hc * 128:(hc + 1) * 128],
                                     "sXNL0", transpose=True)
                    if t > 0:
                        p.wait("scalar", SS2[t - 1]["t_prods"])
                    if t >= 2:
                        p.wait("scalar", ("tGM", t - 1))
                    for sec in range(2):
                        wsrc = (wq_perm, wg_perm)[sec]
                        dst = (q_fm, gate_fm[t % 2])[sec]
                        bias = (bias_q, bias_g)[sec]
                        for mc in range(8):
                            i = sec * 8 + mc
                            ws = i % 2
                            pslot = i % 2
                            p.wait("sync", wqg_free[ws])
                            twl = p.dma("sync", wqg_buf[ws][:],
                                        wsrc[mc].rearrange("h p m -> p h m"),
                                        "sWKV0" if ws == 0 else "sWKV1")
                            p.wait("tensor", [twl, txn2, pbc_free[pslot], t_bias])
                            for hc in range(8):
                                tqg = p.op("tensor",
                                           lambda e, hc=hc, ws=ws, ps=pslot:
                                           e.matmul(out=p_bc[ps][:],
                                                    lhsT=wqg_buf[ws][:, hc, :],
                                                    rhs=xn2[:, hc, :],
                                                    start=(hc == 0), stop=(hc == 7)),
                                           inc="tT" if hc == 7 else None)
                            wqg_free[ws] = tqg
                            p.wait("scalar", tqg)
                            if sec == 0:
                                qg_evac_last = p.op(
                                    "scalar", lambda e, mc=mc, ps=pslot: e.activation(
                                        out=q_fm[:, mc, :], in_=p_bc[ps][:],
                                        func=AF.Identity, bias=bias_q[:, mc:mc + 1],
                                        scale=QSCALE), inc="tC")
                            else:
                                qg_evac_last = p.op(
                                    "scalar", lambda e, mc=mc, ps=pslot, gf=gate_fm[t % 2]:
                                    e.activation(
                                        out=gf[:, mc, :], in_=p_bc[ps][:],
                                        func=AF.Sigmoid, bias=bias_g[:, mc:mc + 1]),
                                    inc="tC")
                            pbc_free[pslot] = qg_evac_last
                    xn2_free = tqg
                    SS2[t] = dict(tq_done=p.tok("tC"))

                def a2a_sc(t):
                    c0 = t * NT
                    tq_done = SS2[t]["tq_done"]
                    for oi, o in enumerate(DIST):
                        g, ol = oi // 3, oi % 3
                        v0 = max(0, min(NT, o - c0))
                        pslot = oi % 2
                        sslot = g % 2
                        if v0 < NT:
                            v0p = min(NT, v0 + (v0 & 1))
                            for c in range(8):
                                pr = c % 2
                                p.wait("vector", [tq_done, prod_free[pr]])
                                if v0 > 0:
                                    p.op("vector", lambda e, pr=pr, v0p=v0p: e.memset(
                                        prod_sb[pr][:, 0:v0p], 0.0))
                                tpr_c = p.op("vector", lambda e, c=c, pr=pr, v0=v0, o=o, c0=c0:
                                             e.tensor_mul(
                                                 out=prod_sb[pr][:, v0:],
                                                 in0=q_fm[:, c, v0:],
                                                 in1=k_fm[:, c, c0 - o + v0:c0 + NT - o]),
                                             inc="tV")
                                p.wait("tensor", [tpr_c, psc_free[pslot]] if c == 0 else tpr_c)
                                tmm = p.op("tensor", lambda e, c=c, pr=pr, ps=pslot:
                                           e.matmul(out=p_sc[ps][:],
                                                    lhsT=sel_b[:],
                                                    rhs=prod_sb[pr][:],
                                                    start=(c == 0), stop=(c == 7)),
                                           inc="tT")
                                prod_free[pr] = tmm
                            tsc = tmm
                        else:
                            tsc = None
                        tze = None
                        if v0 > 0:
                            v0p = v0 + (v0 & 1)
                            p.wait("vector", qkst_free[sslot] if ol == 0 else None)
                            tze = p.op("vector", lambda e, ol=ol, ss=sslot, v0p=v0p:
                                       e.memset(qk_stage[ss][ol * 32:ol * 32 + NH, 0:v0p],
                                                0.0), inc="tV")
                        if v0 < NT:
                            p.wait("scalar", [tsc, tze,
                                              qkst_free[sslot] if ol == 0 else None])
                            tse = p.op("scalar", lambda e, ps=pslot, ol=ol, ss=sslot, v0=v0:
                                       e.activation(
                                           out=qk_stage[ss][ol * 32:ol * 32 + NH, v0:],
                                           in_=p_sc[ps][:, v0:],
                                           func=AF.Identity), inc="tC")
                            psc_free[pslot] = tse
                            tse = [tse, tze]
                        else:
                            tse = tze
                        if debug and t == 0 and oi == 5:
                            p.wait("gpsimd", tse)
                            tqksd = p.dma("gpsimd", dbg["qks"][:], qk_stage[sslot][:], "sDBG2")
                            p.wait("tensor", tqksd)
                        if ol == 2 or oi == len(DIST) - 1:
                            for su in range(SUBS):
                                pq = 0
                                p.wait("tensor", [tse, pqkT_free[pq]])
                                ttr = p.op("tensor", lambda e, ss=sslot, su=su, pq=pq:
                                           e.transpose(out=p_qkT_v(pq),
                                                       in_=qk_stage[ss][:, su * 128:(su + 1) * 128],
                                                       identity=ident_b[0:80, 0:80]),
                                           inc="tT")
                                p.wait("vector", ttr)
                                tqkc = p.op("vector", lambda e, g=g, su=su, pq=pq:
                                            e.tensor_copy(out=qkT_sb[:, su, g, :],
                                                          in_=p_qkT_v(pq)), inc="tV")
                                pqkT_free[pq] = tqkc
                            qkst_free[sslot] = p.tok("tV")
                    SS2[t]["t_prods"] = p.tok("tV")
                    SS2[t]["t_qkT"] = p.tok("tV")

                def a2a_sm(t):
                    nonlocal attnT_free
                    c0 = t * NT
                    t_qkT = SS2[t]["t_qkT"]
                    for su in range(SUBS):
                        p.wait("vector", t_qkT)
                        for i, off in enumerate(OFFSETS):
                            di = DIST_IDX[off]
                            g, ol = di // 3, di % 3
                            p.op("vector", lambda e, i=i, g=g, ol=ol, su=su: e.tensor_add(
                                out=scores_tm[:, :, i],
                                in0=qkT_sb[:, su, g, ol * 32:ol * 32 + NH],
                                in1=pb_bc[:, :, i]))
                        if debug and t == 0 and su == 0:
                            texpn = p.op("vector", lambda e: e.engine_nop(), inc="tV")
                            p.wait("sync", texpn)
                            tscd = p.dma("sync", dbg["sc"][:],
                                  scores_tm[:].rearrange("p h i -> p (h i)"), "sDBG")
                            p.wait("vector", tscd)
                        p.op("vector", lambda e: e.reduce_max(
                            out=sm_stat[:, :, 0], in_=scores_tm[:], axis=AX.X), inc="tV")
                        p.wait("vector", p.tok("tV"))
                        p.op("vector", lambda e: e.tensor_sub(
                            out=scores_tm[:], in0=scores_tm[:],
                            in1=sm_stat[:, :, 0:1].broadcast_to([128, NH, 44])), inc="tV")
                        texp_in = p.tok("tV")
                        p.wait("scalar", texp_in)
                        texp = p.op("scalar", lambda e: e.activation(
                            out=scores_tm[:].rearrange("p h i -> p (h i)"),
                            in_=scores_tm[:].rearrange("p h i -> p (h i)"),
                            func=AF.Exp), inc="tC")
                        p.wait("vector", texp)
                        p.op("vector", lambda e: e.reduce_sum(
                            out=sm_stat[:, :, 1], in_=scores_tm[:], axis=AX.X), inc="tV")
                        p.wait("vector", p.tok("tV"))
                        p.op("vector", lambda e: e.reciprocal(
                            out=sm_stat[:, :, 1], in_=sm_stat[:, :, 1]), inc="tV")
                        p.wait("vector", p.tok("tV"))
                        p.op("vector", lambda e: e.tensor_mul(
                            out=scores_tm[:], in0=scores_tm[:],
                            in1=sm_stat[:, :, 1:2].broadcast_to([128, NH, 44])), inc="tV")
                        p.wait("vector", p.tok("tV"))
                        def _red0(e):
                            with nc.allow_low_precision(reason="attn collapse bf16 ok"):
                                return e.reduce_sum(out=attn_coll[:, 0, :],
                                                    in_=scores_tm[:, :, 0::4], axis=AX.X)
                        p.op("vector", _red0)
                        p.op("vector", lambda e: e.tensor_copy(
                            out=attn_coll[:, 1, :], in_=scores_tm[:, :, 1]))
                        p.op("vector", lambda e: e.tensor_add(
                            out=attn_coll[:, 2:22:2, :],
                            in0=scores_tm[:, :, 5::4].transpose([0, 2, 1]),
                            in1=scores_tm[:, :, 2:40:4].transpose([0, 2, 1])))
                        p.op("vector", lambda e: e.tensor_copy(
                            out=attn_coll[:, 22, :], in_=scores_tm[:, :, 42]))
                        tcl = p.op("vector", lambda e: e.tensor_copy(
                            out=attn_coll[:, 3:24:2, :],
                            in_=scores_tm[:, :, 3::4].transpose([0, 2, 1])), inc="tV")
                        p.wait("tensor", [tcl, attnT_free])
                        for oi in range(len(DIST)):
                            g, ol = oi // 3, oi % 3
                            pq = 0
                            p.wait("tensor", pat_free[pq])
                            tat = p.op("tensor", lambda e, oi=oi, pq=pq: e.transpose(
                                out=p_at_v(pq), in_=attn_coll[:, oi, :],
                                identity=ident_b[:]), inc="tT")
                            p.wait("scalar", tat)
                            pat_free[pq] = p.op("scalar", lambda e, g=g, ol=ol, su=su, pq=pq:
                                                e.copy(out=attnT[ol * 32:ol * 32 + NH, g,
                                                                 su * 128:(su + 1) * 128],
                                                       in_=p_at_v(pq)), inc="tC")
                    SS2[t]["t_at"] = p.tok("tC")

                def a2a_av(t):
                    nonlocal attnT_free, avacc_free
                    c0 = t * NT
                    t_at = SS2[t]["t_at"]
                    p.wait("vector", avacc_free)
                    first = True
                    for oi, o in enumerate(DIST):
                        v0 = max(0, min(NT, o - c0))
                        if v0 >= NT:
                            continue
                        g, ol = oi // 3, oi % 3
                        par = oi % 2
                        p.wait("tensor", [t_at, pav_free[par]])
                        tb = p.op("tensor", lambda e, g=g, ol=ol, par=par: e.matmul(
                            out=p_av[par][:], lhsT=sel2_b[ol * 32:ol * 32 + NH, :],
                            rhs=attnT[ol * 32:ol * 32 + NH, g, :],
                            start=True, stop=True), inc="tT")
                        p.wait("vector", [tb, bcs_free[par]])
                        tbe = p.op("vector", lambda e, par=par: e.tensor_copy(
                            out=bcast_sb[par][:], in_=p_av[par][:]), inc="tV")
                        pav_free[par] = tbe
                        for c in range(8):
                            if first:
                                tav = p.op("vector", lambda e, c=c, par=par, c0=c0:
                                           e.tensor_mul(out=avacc[:, c, :],
                                                        in0=bcast_sb[par][:],
                                                        in1=v_fm[:, c, c0:c0 + NT]),
                                           inc="tV" if c == 7 else None)
                            else:
                                p.op("vector", lambda e, c=c, par=par, v0=v0, o=o, c0=c0:
                                     e.tensor_mul(out=avtmp[:, v0:],
                                                  in0=bcast_sb[par][:, v0:],
                                                  in1=v_fm[:, c, c0 - o + v0:c0 + NT - o]))
                                tav = p.op("vector", lambda e, c=c, v0=v0:
                                           e.tensor_add(out=avacc[:, c, v0:],
                                                        in0=avacc[:, c, v0:],
                                                        in1=avtmp[:, v0:]),
                                           inc="tV" if c == 7 else None)
                        bcs_free[par] = tav
                        first = False
                    if debug and t == 0:
                        p.wait("gpsimd", [p.tok("tV")])
                        for c in range(8):
                            p.dma("gpsimd", dbg["av"][c], avacc[:, c, :], "sDBG2")
                        t_avd = p.tok("sDBG2")
                        p.wait("vector", t_avd)
                    for c in range(8):
                        tgv = p.op("vector", lambda e, c=c, gf=gate_fm[t % 2]: e.tensor_mul(
                            out=avacc[:, c, :], in0=avacc[:, c, :],
                            in1=gf[:, c, :]), inc="tGM" if c == 7 else None)
                    p.wait("sync", tgv)
                    tspill = p.dma("sync",
                                   gav_sp[:, :, c0:c0 + NT].rearrange("c p n -> p c n"),
                                   avacc[:], "sGAV")
                    avacc_free = tspill
                    attnT_free = p.tok("tT")
                    for sub in range(t * SUBS, (t + 1) * SUBS):
                        par = 0
                        p.wait("sync", [t_xnsp_all, xnsc_free[par]])
                        tld = p.dma("sync", xn_sc[par][:],
                                    xn_sp[sub * 128:(sub + 1) * 128, :], "sXC")
                        p.wait("gpsimd", [tld, t_idx_ready])
                        p.op("gpsimd", lambda e, par=par, ti=sub: e.indirect_dma_start(
                            out=expert_x,
                            out_offset=IndirectOffsetOnAxis(ap=idx1u[:, ti:ti + 1], axis=0),
                            in_=xn_sc[par][:], in_offset=None,
                            bounds_check=REG["bc"], oob_is_err=False), inc="sSC", inck=16)
                        ts2 = p.op("gpsimd", lambda e, par=par, ti=sub: e.indirect_dma_start(
                            out=expert_x,
                            out_offset=IndirectOffsetOnAxis(ap=idx2u[:, ti:ti + 1], axis=0),
                            in_=xn_sc[par][:], in_offset=None,
                            bounds_check=REG["bc"], oob_is_err=False), inc="sSC", inck=16)
                        xnsc_free[par] = ts2
                    p.dma("gpsimd", w1_bf[t], w1[t], "qW")
                    p.dma("gpsimd", w2_bf[t], w2[t], "qW")

                a2a_q(0)
                for t in range(NTILES):
                    a2a_sc(t)
                    a2a_sm(t)
                    if t + 1 < NTILES:
                        a2a_q(t + 1)
                    a2a_av(t)

                if debug:
                    p.wait("gpsimd", [p.tok("sGAV")])
                    for c in range(8):
                        p.dma("gpsimd", dbg["gav"][c], gav_sp[c], "sDBG2")

        kv_stack.close()
        p.barrier()
        if "B1" in phases:
            with contextlib.ExitStack() as sC:
                acc = sC.enter_context(sb("acc", [128, 8, HALF], BF16))
                coef_sb = sC.enter_context(sb("coef_sb", [E, N], BF16))
                gav_t = [sC.enter_context(sb(f"gav_t{i}", [128, 8, NT], BF16)) for i in range(2)]
                w_out_sb = sC.enter_context(sb("w_out_sb", [128, 8, D], BF16))
                p_m2 = [sC.enter_context(nc.psum_tensor(f"p_m2{i}", [128, NT], F32)) for i in range(2)]

                wo_r = w_out.rearrange("(hh r c) m -> c (hh r) m", hh=16, r=8, c=8)
                for hc in range(8):
                    t_wo = p.dma("gpsimd", w_out_sb[:, hc, :], wo_r[hc], "sWO")
                t_cf = p.dma("sync", coef_sb[:], coef_sp, "sCF")

                pm2_free = [None, None]
                gav_free = [None, None]
                acc_free = None
                for hf in range(2):
                    H0 = hf * HALF
                    for nt in range(4):
                        gsl = nt % 2
                        p.wait("sync", [p.tok("sGAV"), gav_free[gsl]])
                        tgv = p.dma("sync", gav_t[gsl][:],
                                    gav_sp[:, :, H0 + nt * NT:H0 + (nt + 1) * NT]
                                    .rearrange("c p n -> p c n"),
                                    "sGV0" if gsl == 0 else "sGV1")
                        for mc in range(8):
                            par = (nt * 8 + mc) % 2
                            p.wait("tensor", [tgv, t_wo, t_cf, pm2_free[par], t_bias])
                            for hc in range(8):
                                p.op("tensor", lambda e, hc=hc, mc=mc, gsl=gsl, par=par:
                                     e.matmul(out=p_m2[par][:],
                                              lhsT=w_out_sb[:, hc, mc * 128:(mc + 1) * 128],
                                              rhs=gav_t[gsl][:, hc, :],
                                              start=(hc == 0), stop=False), inc=None)
                            tai = p.op("tensor", lambda e, mc=mc, nt=nt, H0=H0, par=par:
                                       e.matmul(out=p_m2[par][:],
                                                lhsT=b2_bf[:, mc * 128:(mc + 1) * 128],
                                                rhs=coef_sb[:, H0 + nt * NT:H0 + (nt + 1) * NT],
                                                start=False, stop=True), inc="tT")
                            gav_free[gsl] = tai
                            p.wait("scalar", [tai, acc_free])
                            pm2_free[par] = p.op(
                                "scalar", lambda e, mc=mc, nt=nt, par=par: e.activation(
                                    out=acc[:, mc, nt * NT:(nt + 1) * NT],
                                    in_=p_m2[par][:], func=AF.Identity,
                                    bias=bias_o[:, mc:mc + 1]), inc="tC")
                    t_accinit = p.tok("tC")
                    p.wait("sync", t_accinit)
                    t_as = p.dma("sync", attn_sp[:, :, H0:H0 + HALF]
                                 .rearrange("c p n -> p c n"), acc[:], "sAS")
                    acc_free = t_as
            p.barrier()

        if "B2" in phases:
            with contextlib.ExitStack() as sE:
                xe = [sE.enter_context(sb(f"xe{i}", [128, 8, C], BF16)) for i in range(2)]
                w1g = [sE.enter_context(sb(f"w1g{i}", [128, 8, 1024], BF16)) for i in range(2)]
                w2g = sE.enter_context(sb("w2g", [128, 8, 1024], BF16))
                h1_sb = sE.enter_context(sb("h1_sb", [128, 8, NT], BF16))
                m2acc = sE.enter_context(sb("m2acc", [128, 8, C], BF16))
                ye_st = [sE.enter_context(sb(f"ye_st{i}", [128, D], BF16)) for i in range(2)]
                p_h1 = [sE.enter_context(nc.psum_tensor(f"p_h1{i}", [128, NT], F32)) for i in range(2)]
                p_m2e = [sE.enter_context(nc.psum_tensor(f"p_m2e{i}", [128, NT], F32)) for i in range(2)]
                p_ye = [sE.enter_context(nc.psum_tensor(f"p_ye{i}", [128, D], BF16)) for i in range(2)]

                t_wcast_all = p.tok("qW")
                t_scat_all = p.tok("sSC")
                xe_free = [None, None]
                w1_free = [None, None]
                w2_free = None
                ph1_free = [None, None]
                pm2e_free = [None, None]
                pye_free = [None, None]
                yest_free = [None, None]
                h1_free = None
                m2acc_free = None
                macc_tok = [[None] * CT for _ in range(8)]

                for ei in range(E):
                    par = ei % 2
                    p.wait("sync", [t_scat_all, xe_free[par]])
                    for hc in range(8):
                        t_xe = p.dma("sync", xe[par][:, hc, :],
                                     expert_x[ei * C:(ei + 1) * C,
                                              hc * 128:(hc + 1) * 128],
                                     "sEX0" if par == 0 else "sEX1", transpose=True)
                    for g in range(4):
                        wsl = g % 2
                        p.wait("sync", [("qW", 32 * (ei + 1)), w1_free[wsl]])
                        for hc in range(8):
                            t_w1 = p.dma("sync", w1g[wsl][:, hc, :],
                                         w1_bf[ei, g * 1024:(g + 1) * 1024,
                                               hc * 128:(hc + 1) * 128],
                                         "sWG0" if wsl == 0 else "sWG1",
                                         transpose=True)
                        p.wait("sync", w2_free)
                        for fc in range(8):
                            t_w2 = p.dma("sync", w2g[:, fc, :],
                                         w2_bf[ei, :, (g * 8 + fc) * 128:(g * 8 + fc + 1) * 128],
                                         "sWG0" if wsl == 0 else "sWG1",
                                         transpose=True)
                        for nt in range(CT):
                            for fc in range(8):
                                pp = fc % 2
                                p.wait("tensor", [t_w1, t_xe, ph1_free[pp]])
                                for hc in range(8):
                                    th1 = p.op("tensor",
                                               lambda e, hc=hc, fc=fc, nt=nt, wsl=wsl, par=par, pp=pp:
                                               e.matmul(out=p_h1[pp][:],
                                                        lhsT=w1g[wsl][:, hc, fc * 128:(fc + 1) * 128],
                                                        rhs=xe[par][:, hc, nt * NT:(nt + 1) * NT],
                                                        start=(hc == 0), stop=(hc == 7)),
                                               inc="tT" if hc == 7 else None)
                                p.wait("scalar", [th1, h1_free])
                                tg1 = p.op("scalar", lambda e, fc=fc, g=g, ei=ei, pp=pp:
                                           e.activation(out=h1_sb[:, fc, :],
                                                        in_=p_h1[pp][:], func=AF.Gelu,
                                                        bias=bias_b1[:, ei, g * 8 + fc:g * 8 + fc + 1]),
                                           inc="tC")
                                ph1_free[pp] = tg1
                            t_h1 = p.tok("tC")
                            for hc in range(8):
                                pp2 = hc % 2
                                p.wait("tensor", [t_w2, t_h1, pm2e_free[pp2]])
                                for fc in range(8):
                                    tm2 = p.op("tensor",
                                               lambda e, hc=hc, fc=fc, pp2=pp2:
                                               e.matmul(out=p_m2e[pp2][:],
                                                        lhsT=w2g[:, fc, hc * 128:(hc + 1) * 128],
                                                        rhs=h1_sb[:, fc, :],
                                                        start=(fc == 0), stop=(fc == 7)),
                                               inc="tT" if fc == 7 else None)
                                if g == 0:
                                    p.wait("scalar", [tm2, m2acc_free])
                                    ev = p.op("scalar", lambda e, hc=hc, nt=nt, pp2=pp2:
                                              e.activation(out=m2acc[:, hc, nt * NT:(nt + 1) * NT],
                                                           in_=p_m2e[pp2][:],
                                                           func=AF.Identity), inc="tC")
                                else:
                                    p.wait("vector", [tm2, macc_tok[hc][nt]])
                                    ev = p.op("vector", lambda e, hc=hc, nt=nt, pp2=pp2:
                                              e.tensor_add(out=m2acc[:, hc, nt * NT:(nt + 1) * NT],
                                                           in0=m2acc[:, hc, nt * NT:(nt + 1) * NT],
                                                           in1=p_m2e[pp2][:]), inc="tV")
                                macc_tok[hc][nt] = ev
                                pm2e_free[pp2] = ev
                            h1_free = p.tok("tT")
                        w1_free[wsl] = p.tok("tT")
                        w2_free = p.tok("tT")
                    xe_free[par] = p.tok("tT")
                    t_m2all = [p.tok("tV"), p.tok("tC")]
                    for su in range(C // 128):
                        pt = su % 2
                        p.wait("tensor", [t_m2all, pye_free[pt]])
                        for hc in range(8):
                            tt = p.op("tensor", lambda e, hc=hc, su=su, pt=pt:
                                      e.transpose(out=p_ye[pt][:, hc * 128:(hc + 1) * 128],
                                                  in_=m2acc[:, hc, su * 128:(su + 1) * 128],
                                                  identity=ident_b[:]),
                                      inc="tT" if hc == 7 else None)
                        p.wait("vector", [tt, yest_free[pt]])
                        tev = p.op("vector", lambda e, pt=pt: e.tensor_copy(
                            out=ye_st[pt][:], in_=p_ye[pt][:]), inc="tV")
                        pye_free[pt] = tev
                        p.wait("sync", tev)
                        yest_free[pt] = p.dma("sync",
                                              y_sp[ei * C + su * 128:ei * C + (su + 1) * 128, :],
                                              ye_st[pt][:], "sY")
                    m2acc_free = p.tok("tT")
            p.barrier()

        if "B3" in phases:
            with contextlib.ExitStack() as sF:
                at_t = [sF.enter_context(sb(f"at_t{i}", [128, D], BF16)) for i in range(2)]
                ga = [sF.enter_context(sb(f"ga{i}", [128, D], BF16)) for i in range(2)]
                gb = [sF.enter_context(sb(f"gb{i}", [128, D], BF16)) for i in range(2)]
                x_c = [sF.enter_context(sb(f"x_c{i}", [128, D], F32)) for i in range(2)]
                tmpf = [sF.enter_context(sb(f"tmpf{i}", [128, D], F32)) for i in range(2)]

                t_y_all = p.tok("sY")
                t_as_all = p.tok("sAS")
                at_free = [None, None]
                g_free = [None, None]
                xc_free = [None, None]
                tmp_free = [None, None]
                for ti in range(NSUB):
                    par = ti % 2
                    p.wait("sync", [t_as_all, at_free[par]])
                    t_at = p.dma("sync", at_t[par][:],
                                 attn_sp[:, :, ti * 128:(ti + 1) * 128]
                                 .rearrange("c p n -> (c p) n"),
                                 "sAL", transpose=True)
                    p.wait("sync", xc_free[par])
                    t_x = p.dma("sync", x_c[par][:], x[ti * 128:(ti + 1) * 128, :],
                                "sX0" if par == 0 else "sX1")
                    p.wait("gpsimd", [t_y_all, g_free[par]])
                    p.op("gpsimd", lambda e, par=par, ti=ti: e.indirect_dma_start(
                        out=ga[par][:], out_offset=None,
                        in_=y_sp,
                        in_offset=IndirectOffsetOnAxis(ap=idx1u[:, ti:ti + 1], axis=0),
                        bounds_check=REG["bc"], oob_is_err=False), inc="sGA", inck=16)
                    tgb = p.op("gpsimd", lambda e, par=par, ti=ti: e.indirect_dma_start(
                        out=gb[par][:], out_offset=None,
                        in_=y_sp,
                        in_offset=IndirectOffsetOnAxis(ap=idx2u[:, ti:ti + 1], axis=0),
                        bounds_check=REG["bc"], oob_is_err=False), inc="sGA", inck=16)
                    p.wait("vector", [tgb, tmp_free[par]])
                    v1 = p.op("vector", lambda e, par=par, ti=ti: e.tensor_scalar(
                        out=tmpf[par][:], in0=ga[par][:], scalar1=wab[:, ti, 0:1],
                        scalar2=None, op0=ALU.mult), inc="tV")
                    p.wait("vector", v1)
                    v2 = p.op("vector", lambda e, par=par, ti=ti: e.scalar_tensor_tensor(
                        out=tmpf[par][:], in0=gb[par][:], scalar=wab[:, ti, 1:2],
                        in1=tmpf[par][:], op0=ALU.mult, op1=ALU.add), inc="tV")
                    g_free[par] = v2
                    p.wait("vector", [v2, t_at])
                    v3 = p.op("vector", lambda e, par=par: e.tensor_add(
                        out=tmpf[par][:], in0=tmpf[par][:], in1=at_t[par][:]), inc="tV")
                    at_free[par] = v3
                    p.wait("vector", [v3, t_x])
                    v4 = p.op("vector", lambda e, par=par: e.tensor_add(
                        out=x_c[par][:], in0=x_c[par][:], in1=tmpf[par][:]), inc="tV")
                    tmp_free[par] = v4
                    p.wait("sync", v4)
                    t_out = p.dma("sync", out[ti * 128:(ti + 1) * 128, :],
                                  x_c[par][:], "sOUT")
                    xc_free[par] = [v4, t_out]

        p.barrier()
        p.wait("sync", [(sm, p.n[sm]) for sm in p.n])
        p.check_deadlock()
        p.emit(blk)
    return nc, dbg


def kernel(**inputs):
    x = np.ascontiguousarray(np.asarray(inputs["L0_attn_x"]))
    weights = {
        "w_qkv": np.ascontiguousarray(np.asarray(inputs["L0_attn_w_qkv"])),
        "b_qkv": np.ascontiguousarray(np.asarray(inputs["L0_attn_b_qkv"])),
        "w_out": np.ascontiguousarray(np.asarray(inputs["L0_attn_w_out"])),
        "b_out": np.ascontiguousarray(np.asarray(inputs["L0_attn_b_out"])),
        "w_gate": np.ascontiguousarray(np.asarray(inputs["L0_attn_w_gate"])),
        "b_gate": np.ascontiguousarray(np.asarray(inputs["L0_attn_b_gate"])),
        "pos_bias": np.ascontiguousarray(np.asarray(inputs["L0_attn_pos_bias"])),
        "router_w": np.ascontiguousarray(np.asarray(inputs["L0_ffn_router_w"])),
        "w1": np.ascontiguousarray(np.asarray(inputs["L0_ffn_w1"])),
        "b1": np.ascontiguousarray(np.asarray(inputs["L0_ffn_b1"])),
        "w2": np.ascontiguousarray(np.asarray(inputs["L0_ffn_w2"])),
        "b2": np.ascontiguousarray(np.asarray(inputs["L0_ffn_b2"])),
    }
    nc, _dbg = build_nc(debug=False)
    in_maps = [dict(weights, x=np.ascontiguousarray(x[b])) for b in range(8)]
    res = run_bass_kernel_spmd(nc, in_maps, core_ids=list(range(8)))
    return np.stack([res.results[b]["out"] for b in range(8)], axis=0)
